# version 08
# baseline (speedup 1.0000x reference)
import contextlib

import numpy as np
import ml_dtypes

import concourse.bass as bass
import concourse.mybir as mybir
from concourse.bass_utils import run_bass_kernel_spmd

F32 = mybir.dt.float32
BF16 = mybir.dt.bfloat16
AF = mybir.ActivationFunctionType
ALU = mybir.AluOpType
AX = mybir.AxisListType

NCORES = 8
T = 1024
NCH = 8
DM = 2048
DI = 4096
DXB = 1024
DST = 64
HD = 64
NH = 64
NG = 16
DCONV = 4
CONV_DIM = 6144
DFF = 8192
NEXP = 8
DEXP = 1408
EPS = 1e-5

STREAMS = ("pe", "dve", "act", "pool", "sp")


class Emitter:

    NDMA = {"sp": 8, "pool": 8, "act": 2}

    def __init__(self, nc, stack):
        self.nc = nc
        self.ops = {s: [] for s in STREAMS}
        self.chans = list(STREAMS) + ["cc"]
        for st, n in self.NDMA.items():
            self.chans += [f"{st}_d{i}" for i in range(n)]
        self.sems = {ch: stack.enter_context(nc.semaphore(f"sem_{ch}"))
                     for ch in self.chans}
        self.count = {ch: 0 for ch in self.chans}
        self.dma_rr = {st: 0 for st in self.NDMA}
        self.res = {}
        self.wm = {s: {ch: 0 for ch in self.chans} for s in STREAMS}
        self.snap = {ch: {0: {c2: 0 for c2 in self.chans}} for ch in self.chans}
        self.pending = {s: [] for s in STREAMS}
        self.oplog = []

    def _wait(self, stream, chan, val, is_dma):
        if val <= 0:
            return
        if self.wm[stream].get(chan, 0) >= val:
            return
        sem = self.sems[chan]
        self.ops[stream].append(lambda eng, sem=sem, v=val: eng.wait_ge(sem, v))
        self.wm[stream][chan] = val
        snap = self.snap[chan]
        best = None
        for v in snap:
            if v <= val and (best is None or v > best):
                best = v
        if best is not None:
            for c2, v2 in snap[best].items():
                if self.wm[stream].get(c2, 0) < v2:
                    self.wm[stream][c2] = v2

    def op(self, stream, fn, reads=(), writes=(), inc=1, dma=False, cc=False):
        if cc:
            chan = "cc"
            inc = 1
        elif dma:
            i = self.dma_rr[stream] % self.NDMA[stream]
            self.dma_rr[stream] += 1
            chan = f"{stream}_d{i}"
            inc = 16
        else:
            chan = stream
        for key in reads:
            r = self.res.get(key)
            if r is None:
                continue
            ch, c, isd = r["w"]
            if ch is not None:
                assert c is not None, f"read {key}: producer on {ch} lacks inc"
                self._wait(stream, ch, c, isd)
        for key in writes:
            for s2 in STREAMS:
                if s2 != stream:
                    for pk, pw in self.pending[s2]:
                        assert pk != key, (
                            f"write of {key} races pending (inc=0) access on {s2}")
            r = self.res.get(key)
            if r is None:
                continue
            ch, c, isd = r["w"]
            if ch is not None:
                assert c is not None, f"write {key}: producer lacks inc"
                self._wait(stream, ch, c, isd)
            for ch2, c2, isd2 in r["r"]:
                assert c2 is not None, f"write {key}: reader on {ch2} lacks inc"
                self._wait(stream, ch2, c2, isd2)

        if inc:
            if dma or cc:
                self._wait(stream, chan, self.count[chan], True)
            newc = self.count[chan] + inc
            sem = self.sems[chan]

            def wrapped(eng, fn=fn, sem=sem, inc=inc):
                ins = fn(eng)
                assert ins is not None, "op with inc must return an instruction"
                ins.then_inc(sem, inc)

            import traceback
            fr = traceback.extract_stack(limit=3)[0]
            self.oplog.append((chan, newc, f"{fr.filename.split('/')[-1]}:{fr.lineno}"))
            self.ops[stream].append(wrapped)
            self.count[chan] = newc
            for key, is_w in self.pending[stream]:
                self._record(key, is_w, chan, newc, dma)
            self.pending[stream] = []
            for key in reads:
                self._record(key, False, chan, newc, dma)
            for key in writes:
                self._record(key, True, chan, newc, dma)
            self.snap[chan][newc] = dict(self.wm[stream])
        else:
            assert not dma
            self.ops[stream].append(lambda eng, fn=fn: fn(eng))
            for key in reads:
                self.pending[stream].append((key, False))
            for key in writes:
                self.pending[stream].append((key, True))

    def _record(self, key, is_write, chan, count, is_dma):
        r = self.res.setdefault(key, {"w": (None, None, False), "r": []})
        if is_write:
            r["w"] = (chan, count, is_dma)
            r["r"] = []
        else:
            r["r"].append((chan, count, is_dma))

    def barrier(self, streams=STREAMS, skip_cc=False):
        vals = {}
        for s in streams:
            sem = self.sems[s]
            self.count[s] += 1
            c = self.count[s]
            self.ops[s].append(lambda eng, sem=sem: eng.drain(fusable=False).then_inc(sem, 1))
            self.snap[s][c] = dict(self.wm[s])
            vals[s] = c
        dma_chans = [ch for ch in self.chans
                     if "_d" in ch or (ch == "cc" and not skip_cc)]
        dma_vals = {ch: self.count[ch] for ch in dma_chans}
        for s in streams:
            for s2 in streams:
                if s2 != s:
                    self._wait(s, s2, vals[s2], False)
            for ch, v in dma_vals.items():
                self._wait(s, ch, v, True)


def build_consts():
    cf = {}
    cb = {}
    colf = []
    colb = []

    def addf(name, arr):
        arr = np.asarray(arr, np.float32)
        cf[name] = (sum(a.shape[1] for a in colf), arr.shape[1])
        colf.append(arr)

    def addb(name, arr):
        arr = np.asarray(arr, np.float32).astype(ml_dtypes.bfloat16)
        cb[name] = (sum(a.shape[1] for a in colb), arr.shape[1])
        colb.append(arr)

    ident = np.eye(128, dtype=np.float32)
    addf("identf", ident)
    addb("identb", ident)
    s = np.arange(128)
    addf("tril", (s[:, None] <= s[None, :]).astype(np.float32))
    onesr = np.zeros((128, 128), np.float32)
    onesr[0, :] = 1.0
    addf("onesrow", onesr)
    addf("onescol", np.ones((128, 1), np.float32))
    addf("epscol", np.full((128, 1), EPS, np.float32))
    addf("onecol", np.ones((128, 1), np.float32))

    sel8 = np.zeros((128, 8 * 128), np.float32)
    for e in range(8):
        sel8[e, e * 128:(e + 1) * 128] = 1.0
    addf("sel8", sel8)
    addf("zeros32", np.zeros((128, 32), np.float32))
    trilneg = np.where(np.arange(128)[:, None] > np.arange(128)[None, :],
                       np.float32(-1e30), np.float32(0.0))
    addf("trilneg", trilneg.astype(np.float32))
    return np.concatenate(colf, 1), cf, np.concatenate(colb, 1), cb


class Builder:
    def __init__(self, debug=False, n_cores=NCORES, upto=99, p2_chunks=NCH, p2_tail=True):
        self.debug = debug
        self.n_cores = n_cores
        self.upto = upto
        self.p2_chunks = p2_chunks
        self.p2_tail = p2_tail
        self.p2_parts = 7
        self.p3_chunks = NCH
        self.p3_parts = 0xfff
        self.CF, self.cfm, self.CB, self.cbm = build_consts()
        self.nc = bass.Bass("TRN2", target_bir_lowering=False)

    def cfv(self, name, rows=128):
        off, w = self.cfm[name]
        return self.scf[0:rows, off:off + w]

    def cbv(self, name, rows=128):
        off, w = self.cbm[name]
        return self.scb[0:rows, off:off + w]

    def selv(self, h):
        return self.ssel[0:64, h * 128:(h + 1) * 128]

    def sel8v(self, e):
        off, _ = self.cfm["sel8"]
        return self.scf[0:8, off + e * 128: off + (e + 1) * 128]

    def stat(self, c, i):
        return self.stats[:, c * 8 + i: c * 8 + i + 1]

    def build(self):
        nc = self.nc
        dbg = self.debug

        def din(name, shape, dt=F32):
            return nc.dram_tensor(name, list(shape), dt, kind="ExternalInput").ap()

        self.x_own = din("x_own", [T, DM])
        self.x_ctx = din("x_ctx", [128, DM])
        self.w_in_p1 = din("w_in_p1", [48, 128, 2048])
        self.w_in_dt = din("w_in_dt", [128, 1024])
        self.w_in_z = din("w_in_z", [8, 16, 128, 512])
        self.conv_w = din("conv_w", [128, 48 * DCONV])
        self.conv_b = din("conv_b", [128, 48])
        self.dt_bias = din("dt_bias", [NH, 1])
        self.a_log = din("a_log", [NH, 1])
        self.d_vec = din("d_vec", [1, NH])
        self.norm_w = din("norm_w", [128, 32])
        self.w_out_t = din("w_out_t", [4, 32, 128, 512])
        self.ln1_w = din("ln1_w", [1, DM])
        self.ln2_w = din("ln2_w", [1, DM])
        self.w_gate_t = din("w_gate_t", [64, 128, 2048])
        self.w_up_t = din("w_up_t", [64, 128, 2048])
        self.w_down_t = din("w_down_t", [4, 32, 128, 1024])
        self.wr = din("wr", [128, 16 * NEXP])
        self.wg_t = din("wg_t", [NEXP, 11, 128, 2048])
        self.wu_t = din("wu_t", [NEXP, 11, 128, 2048])
        self.wd_t = din("wd_t", [4, 44, 128, 1024])
        self.constf = din("constf", [128, self.CF.shape[1]])
        self.constsel = din("constsel", [128, 64 * 128])
        self.constb = din("constb", [128, self.CB.shape[1]], BF16)
        self.foldmask = din("foldmask", [1, 16])
        self.out = nc.dram_tensor("out", [T, DM], F32, kind="ExternalOutput").ap()

        self.conv_spill = nc.dram_tensor("conv_spill", [CONV_DIM, T], F32, kind="Internal").ap()
        self.z_spill = nc.dram_tensor("z_spill", [T, DI], F32, kind="Internal").ap()
        self.yfm_spill = nc.dram_tensor("yfm_spill", [DI, T], BF16, kind="Internal").ap()
        self.r1_spill = nc.dram_tensor("r1_spill", [T, DM], F32, kind="Internal").ap()
        self.a1_spill = nc.dram_tensor("a1_spill", [T, DM], F32, kind="Internal").ap()
        self.cc_in = nc.dram_tensor("cc_in", [65, 4096], F32, kind="Internal")
        self.cc_out = nc.dram_tensor("cc_out", [self.n_cores * 65, 4096], F32, kind="Internal")

        self.dbg = {}
        if dbg:
            self.dbg["conv"] = nc.dram_tensor("dbg_conv", [CONV_DIM, T], F32, kind="ExternalOutput").ap()
            self.dbg["dt"] = nc.dram_tensor("dbg_dt", [NH, T], F32, kind="ExternalOutput").ap()
            self.dbg["z"] = nc.dram_tensor("dbg_z", [T, DI], F32, kind="ExternalOutput").ap()
            self.dbg["y"] = nc.dram_tensor("dbg_y", [DI, T], BF16, kind="ExternalOutput").ap()
            self.dbg["r1"] = nc.dram_tensor("dbg_r1", [T, DM], F32, kind="ExternalOutput").ap()
            self.dbg["a1"] = nc.dram_tensor("dbg_a1", [T, DM], F32, kind="ExternalOutput").ap()
            self.dbg["prevs"] = nc.dram_tensor("dbg_prevs", [64, NH * HD], F32, kind="ExternalOutput").ap()
            self.dbg["xfmctx"] = nc.dram_tensor("dbg_xfmctx", [128, 16 * 128], F32, kind="ExternalOutput").ap()
            self.dbg["scc"] = nc.dram_tensor("dbg_scc", [128, 12], F32, kind="ExternalOutput").ap()
            self.dbg["dacT"] = nc.dram_tensor("dbg_dacT", [128, 512], F32, kind="ExternalOutput").ap()
            self.dbg["edacT"] = nc.dram_tensor("dbg_edacT", [128, 512], F32, kind="ExternalOutput").ap()
            self.dbg["dachm"] = nc.dram_tensor("dbg_dachm", [64, 1024], F32, kind="ExternalOutput").ap()
            self.dbg["t1b"] = nc.dram_tensor("dbg_t1b", [128, DM], F32, kind="ExternalOutput").ap()
            self.dbg["dts"] = nc.dram_tensor("dbg_dts", [64, 3 * T], F32, kind="ExternalOutput").ap()

        with contextlib.ExitStack() as stack:
            self.stack = stack
            em = Emitter(nc, stack)
            self.em = em

            def sbuf(name, shape, dt=F32):
                return stack.enter_context(nc.sbuf_tensor(name, list(shape), dt))

            def psum(name, shape, dt=F32):
                return stack.enter_context(nc.psum_tensor(name, list(shape), dt))

            self.sbuf = sbuf
            self.psum = psum

            self.scf = sbuf("scf", [128, self.CF.shape[1]])
            self.scb = sbuf("scb", [128, self.CB.shape[1]], BF16)
            self.stats = sbuf("stats", [128, 8 * NCH])
            self.normw_sb = sbuf("normw_sb", [128, 32])
            self.lnw_b = sbuf("lnw_b", [128, DM])
            self.ssdstack = contextlib.ExitStack()
            def ssdbuf(name, shape, dt=F32):
                return self.ssdstack.enter_context(nc.sbuf_tensor(name, list(shape), dt))
            self.dt_hm = ssdbuf("dt_hm", [64, T])
            self.acol = ssdbuf("acol", [64, 4])
            self.dbcast = ssdbuf("dbcast", [128, NH])
            self.dAcT = ssdbuf("dAcT", [128, NCH * NH])
            self.ndAcT = ssdbuf("ndAcT", [128, NCH * NH])
            self.edAcT = ssdbuf("edAcT", [128, NCH * NH])
            self.dtT = ssdbuf("dtT", [128, NCH * NH])
            self.decT = ssdbuf("decT", [128, NCH * NH])
            self.cdb = ssdbuf("cdb", [64, NCH * NH])
            self.dAc_hm = ssdbuf("dAc_hm", [64, NCH * 128])
            self.prevs = [ssdbuf("prevs0", [64, NH * HD]), ssdbuf("prevs1", [64, NH * HD])]
            self.sumlast = ssdbuf("sumlast", [1, 128])
            self.ssel = ssdbuf("ssel", [128, 64 * 128])

            self.pb = [psum(f"pb{i}", [128, 512]) for i in range(6)]
            self.pt = [psum(f"pt{i}", [128, 1024], BF16) for i in range(2)]
            self.PB = [f"pb{i}" for i in range(6)]
            self.PT = [f"pt{i}" for i in range(2)]

            self.xstack = contextlib.ExitStack()
            self.x_fm = self.xstack.enter_context(
                nc.sbuf_tensor("x_fm", [128, 16 * T], BF16))
            self.xfm_ctx = self.xstack.enter_context(
                nc.sbuf_tensor("xfm_ctx", [128, 16 * 128], BF16))

            em.op("sp", lambda e: e.dma_start(out=self.scf[:], in_=self.constf),
                  writes=["scf"], dma=True)
            em.op("sp", lambda e: e.dma_start(out=self.scb[:], in_=self.constb),
                  writes=["scb"], dma=True)
            em.op("sp", lambda e: e.dma_start(out=self.ssel[:], in_=self.constsel),
                  writes=["ssel"], dma=True)

            if not getattr(self, "skip_p0", False):
                self.phase0()
            if not getattr(self, "skip_p1", False):
                self.phase1()
            if self.upto >= 2:
                self.phase2()
            self.phase2fold()
            if self.upto >= 4:
                self.phase3()
            else:
                self.xstack.close()
            if self.upto >= 5:
                self.phase5()
            else:
                self.ssdstack.close()
            if self.upto >= 6:
                self.phase6()
            self.finish()
        return nc

    def phase0(self):
        em, nc = self.em, self.nc
        pb, pt, PB, PT = self.pb, self.pt, self.PB, self.PT
        with contextlib.ExitStack() as ph:
            def psb(name, shape, dt=F32):
                return ph.enter_context(nc.sbuf_tensor(name, list(shape), dt))

            a0b = psb("a0b", [128, 2 * DM])
            t1f = psb("t1f", [128, DM])
            t1b = psb("t1b", [128, DM], BF16)
            sqs = psb("sqs", [128, DM])
            scc = psb("scc", [128, 12])
            lnrow = psb("lnrow", [1, 2 * DM])
            ln1b = psb("ln1b", [128, DM])

            em.op("sp", lambda e: e.dma_start(out=lnrow[0:1, 0:DM], in_=self.ln1_w),
                  writes=["lnrow"], dma=True)
            em.op("sp", lambda e: e.dma_start(out=lnrow[0:1, DM:2 * DM], in_=self.ln2_w),
                  writes=["lnrow"], dma=True)
            for j in range(8):
                em.op("pe", lambda e, j=j: e.matmul(
                    pb[j % 2][:], self.cfv("onesrow")[0:1, :],
                    lnrow[0:1, j * 512:(j + 1) * 512], start=True, stop=True),
                    reads=["scf", "lnrow"], writes=[PB[j % 2]])
                if j < 4:
                    em.op("dve", lambda e, j=j: e.tensor_copy(
                        ln1b[:, j * 512:(j + 1) * 512], pb[j % 2][:]),
                        reads=[PB[j % 2]], writes=["ln1b"])
                else:
                    em.op("dve", lambda e, j=j: e.tensor_copy(
                        self.lnw_b[:, (j - 4) * 512:(j - 3) * 512], pb[j % 2][:]),
                        reads=[PB[j % 2]], writes=["lnw_b"])

            for c in range(NCH + 1):
                ctx = (c == NCH)
                bi = c % 2
                buf = a0b[:, bi * DM:(bi + 1) * DM]
                bufk = f"a0b{bi}"
                src = self.x_ctx if ctx else self.x_own[c * 128:(c + 1) * 128, :]
                em.op("sp", lambda e, buf=buf, src=src: e.dma_start(out=buf, in_=src),
                      writes=[bufk], dma=True)
                sm = scc[:, 0:1]; sq = scc[:, 1:2]
                mcol = scc[:, 2:3]; vcol = scc[:, 3:4]; ve = scc[:, 4:5]
                r1c = scc[:, 5:6]; rcomb = scc[:, 6:7]; tmp = scc[:, 7:8]
                ve2 = scc[:, 8:9]; ve3 = scc[:, 9:10]; ve4 = scc[:, 10:11]; ve5 = scc[:, 11:12]
                em.op("act", lambda e, buf=buf, sm=sm: e.activation(
                    sqs[:], buf, AF.Copy, accum_out=sm),
                    reads=[bufk], writes=["sqs", "scc"])
                em.op("act", lambda e, buf=buf, sq=sq: e.activation(
                    sqs[:], buf, AF.Square, accum_out=sq),
                    reads=[bufk], writes=["sqs", "scc"])
                em.op("dve", lambda e, sm=sm, mcol=mcol: e.tensor_scalar(
                    mcol, sm, 1.0 / DM, None, ALU.mult), reads=["scc"], writes=["scc"])
                em.op("dve", lambda e, mcol=mcol, tmp=tmp: e.tensor_mul(tmp, mcol, mcol),
                      reads=["scc"], writes=["scc"])
                em.op("dve", lambda e, sq=sq, tmp=tmp, vcol=vcol: e.scalar_tensor_tensor(
                    vcol, sq, 1.0 / DM, tmp, ALU.mult, ALU.subtract),
                    reads=["scc"], writes=["scc"])
                em.op("act", lambda e, vcol=vcol, tmp=tmp: e.activation(
                    tmp, vcol, AF.Sqrt, bias=self.cfv("epscol")),
                    reads=["scc", "scf"], writes=["scc"])
                em.op("dve", lambda e, tmp=tmp, r1c=r1c: e.reciprocal(r1c, tmp),
                      reads=["scc"], writes=["scc"])
                em.op("dve", lambda e, vcol=vcol, ve=ve: e.tensor_scalar(
                    ve, vcol, EPS, None, ALU.add), reads=["scc"], writes=["scc"])
                em.op("dve", lambda e, ve=ve, ve2=ve2: e.reciprocal(ve2, ve), reads=["scc"], writes=["scc"])
                em.op("dve", lambda e, vcol=vcol, ve2=ve2, ve3=ve3: e.tensor_mul(ve3, vcol, ve2),
                      reads=["scc"], writes=["scc"])
                em.op("act", lambda e, ve3=ve3, ve4=ve4: e.activation(
                    ve4, ve3, AF.Sqrt, bias=self.cfv("epscol")),
                    reads=["scc", "scf"], writes=["scc"])
                em.op("dve", lambda e, ve4=ve4, ve5=ve5: e.reciprocal(ve5, ve4), reads=["scc"], writes=["scc"])
                em.op("dve", lambda e, r1c=r1c, ve5=ve5, rcomb=rcomb: e.tensor_mul(rcomb, r1c, ve5),
                      reads=["scc"], writes=["scc"])
                if not ctx:
                    em.op("dve", lambda e, mcol=mcol, c=c: e.tensor_copy(self.stat(c, 0), mcol),
                          reads=["scc"], writes=["stats"])
                    em.op("dve", lambda e, r1c=r1c, c=c: e.tensor_copy(self.stat(c, 1), r1c),
                          reads=["scc"], writes=["stats"])
                em.op("dve", lambda e, buf=buf, mcol=mcol, rcomb=rcomb: e.tensor_scalar(
                    t1f[:], buf, mcol, rcomb, ALU.subtract, ALU.mult),
                    reads=[bufk, "scc"], writes=["t1f"])
                em.op("dve", lambda e: e.tensor_mul(t1b[:], t1f[:], ln1b[:]),
                      reads=["t1f", "ln1b"], writes=["t1b"])
                if ctx and self.debug:
                    em.op("pool", lambda e: e.dma_start(out=self.dbg["scc"], in_=scc[:]),
                          reads=["scc"], dma=True)
                    em.op("pool", lambda e: e.dma_start(out=self.dbg["t1b"], in_=t1b[:]),
                          reads=["t1b"], dma=True)
                for fc in range(16):
                    ptk = PT[fc % 2]
                    em.op("pe", lambda e, fc=fc: e.transpose(
                        pt[fc % 2][:, 0:128], t1b[:, fc * 128:(fc + 1) * 128],
                        self.cbv("identb")),
                        reads=["t1b", "scb"], writes=[ptk])
                    if ctx:
                        em.op("dve", lambda e, fc=fc: e.tensor_copy(
                            self.xfm_ctx[:, fc * 128:(fc + 1) * 128], pt[fc % 2][:, 0:128]),
                            reads=[ptk], writes=["xfm_ctx"])
                    else:
                        em.op("dve", lambda e, fc=fc, c=c: e.tensor_copy(
                            self.x_fm[:, fc * T + c * 128: fc * T + (c + 1) * 128],
                            pt[fc % 2][:, 0:128]),
                            reads=[ptk], writes=["x_fm"])
            em.barrier()

    def phase1(self):
        em, nc = self.em, self.nc
        pb, PB = self.pb, self.PB
        with contextlib.ExitStack() as ph:
            def psb(name, shape, dt=F32):
                return ph.enter_context(nc.sbuf_tensor(name, list(shape), dt))

            wb = [psb(f"wb{i}", [128, 16 * 128], BF16) for i in range(2)]
            convin = [psb(f"ci{i}", [128, 1152]) for i in range(2)]
            cvt = psb("cvt", [128, T])
            cvo = [psb(f"cvo{i}", [128, T]) for i in range(2)]
            cw = psb("cw", [128, 48 * DCONV])
            cb_ = psb("cb", [128, 48])
            drow = psb("drow", [1, NH])
            dtraw = psb("dtraw", [64, T])
            dts = psb("dts", [64, 3 * T])

            em.op("sp", lambda e: e.dma_start(out=cw[:], in_=self.conv_w),
                  writes=["cw"], dma=True)
            em.op("sp", lambda e: e.dma_start(out=cb_[:], in_=self.conv_b),
                  writes=["cb"], dma=True)
            em.op("sp", lambda e: e.dma_start(out=self.acol[:, 0:1], in_=self.a_log),
                  writes=["acol"], dma=True)
            em.op("sp", lambda e: e.dma_start(out=self.acol[:, 2:3], in_=self.dt_bias),
                  writes=["acol"], dma=True)
            em.op("sp", lambda e: e.dma_start(out=drow[:], in_=self.d_vec),
                  writes=["drow"], dma=True)
            em.op("act", lambda e: e.activation(self.acol[:, 1:2], self.acol[:, 0:1], AF.Exp),
                  reads=["acol"], writes=["acol"])
            em.op("dve", lambda e: e.tensor_scalar(
                self.acol[:, 1:2], self.acol[:, 1:2], -1.0, None, ALU.mult),
                reads=["acol"], writes=["acol"])
            em.op("pe", lambda e: e.matmul(pb[5][:, 0:NH], self.cfv("onesrow")[0:1, :],
                                           drow[0:1, :], start=True, stop=True),
                  reads=["scf", "drow"], writes=[PB[5]])
            em.op("dve", lambda e: e.tensor_copy(self.dbcast[:], pb[5][:, 0:NH]),
                  reads=[PB[5]], writes=["dbcast"])

            for oc in range(49):
                is_dt = (oc == 48)
                col0 = 4096 + oc * 128 if not is_dt else 10240
                ncol = 128 if not is_dt else 64
                bi = oc % 2
                wkey = f"wb{bi}"
                if is_dt:
                    em.op("pool", lambda e, bi=bi: e.dma_start(
                        out=wb[bi][:, 0:1024], in_=self.w_in_dt),
                        writes=[wkey], dma=True)
                else:
                    em.op("pool", lambda e, bi=bi, oc=oc: e.dma_start(
                        out=wb[bi][:], in_=self.w_in_p1[oc]),
                        writes=[wkey], dma=True)
                b0 = 0 if bi == 0 else 3
                bA, bB, bC = pb[b0], pb[b0 + 1], pb[b0 + 2]
                kA, kB, kC = PB[b0], PB[b0 + 1], PB[b0 + 2]
                for k in range(16):
                    st = (k == 0)
                    sp_ = (k == 15)
                    lhs = lambda k=k, bi=bi, ncol=ncol: wb[bi][:, k * ncol:(k + 1) * ncol]
                    em.op("pe", lambda e, k=k, lhs=lhs, bA=bA, ncol=ncol, st=st, sp_=sp_: e.matmul(
                        bA[0:ncol, 0:512], lhs(), self.x_fm[:, k * T:k * T + 512],
                        start=st, stop=sp_),
                        reads=[wkey, "x_fm"], writes=[kA], inc=0)
                    em.op("pe", lambda e, k=k, lhs=lhs, bB=bB, ncol=ncol, st=st, sp_=sp_: e.matmul(
                        bB[0:ncol, 0:512], lhs(), self.x_fm[:, k * T + 512:k * T + 1024],
                        start=st, stop=sp_),
                        reads=[wkey, "x_fm"], writes=[kB],
                        inc=(1 if (sp_ and is_dt) else 0))
                    if not is_dt:
                        em.op("pe", lambda e, k=k, lhs=lhs, bC=bC, st=st, sp_=sp_: e.matmul(
                            bC[0:128, 0:128], lhs(), self.xfm_ctx[:, k * 128:(k + 1) * 128],
                            start=st, stop=sp_),
                            reads=[wkey, "xfm_ctx"], writes=[kC],
                            inc=(1 if sp_ else 0))

                if is_dt:
                    em.op("dve", lambda e: e.tensor_scalar(
                        dtraw[:, 0:512], bA[0:64, 0:512], self.acol[0:64, 2:3], None, ALU.add),
                        reads=[kA, "acol"], writes=["dtraw"])
                    em.op("dve", lambda e: e.tensor_scalar(
                        dtraw[:, 512:1024], bB[0:64, 0:512], self.acol[0:64, 2:3], None, ALU.add),
                        reads=[kB, "acol"], writes=["dtraw"])
                    r0 = dts[:, 0:T]
                    r1_ = dts[:, T:2 * T]
                    em.op("act", lambda e, r0=r0: e.activation(r0, dtraw[:], AF.Abs),
                          reads=["dtraw"], writes=["dts0"])
                    em.op("act", lambda e, r0=r0, r1_=r1_: e.activation(r1_, r0, AF.Exp, scale=-1.0),
                          reads=["dts0"], writes=["dts1"])
                    em.op("act", lambda e, r0=r0, r1_=r1_: e.activation(
                        r0, r1_, AF.Ln, bias=self.cfv("onecol", 64)),
                        reads=["dts1", "scf"], writes=["dts0"])
                    r2 = dts[:, 2 * T:3 * T]
                    em.op("dve", lambda e, r2=r2: e.tensor_scalar(
                        r2, dtraw[:], 0.0, None, ALU.max), reads=["dtraw"], writes=["dts2"])
                    em.op("dve", lambda e, r0=r0, r2=r2: e.tensor_add(self.dt_hm[:], r2, r0),
                          reads=["dts0", "dts2"], writes=["dt_hm"])
                else:
                    ck = f"ci{bi}"
                    em.op("act", lambda e, bi=bi, bC=bC: e.activation(
                        convin[bi][:, 0:128], bC[0:128, 0:128], AF.Copy),
                        reads=[kC], writes=[ck])
                    em.op("act", lambda e, bi=bi, bA=bA: e.activation(
                        convin[bi][:, 128:640], bA[0:128, 0:512], AF.Copy),
                        reads=[kA], writes=[ck])
                    em.op("act", lambda e, bi=bi, bB=bB: e.activation(
                        convin[bi][:, 640:1152], bB[0:128, 0:512], AF.Copy),
                        reads=[kB], writes=[ck])
                    wc = lambda k, oc=oc: cw[:, oc * DCONV + k: oc * DCONV + k + 1]
                    bcol = cb_[:, oc:oc + 1]
                    em.op("dve", lambda e, bi=bi, wc=wc, bcol=bcol: e.tensor_scalar(
                        cvt[:], convin[bi][:, 125:1149], wc(0), bcol, ALU.mult, ALU.add),
                        reads=[ck, "cw", "cb"], writes=["cvt"])
                    for k in range(1, 4):
                        em.op("dve", lambda e, bi=bi, k=k, wc=wc: e.scalar_tensor_tensor(
                            cvt[:], convin[bi][:, 125 + k:1149 + k], wc(k), cvt[:],
                            ALU.mult, ALU.add),
                            reads=[ck, "cw", "cvt"], writes=["cvt"])
                    em.op("act", lambda e, bi=bi: e.activation(cvo[bi][:], cvt[:], AF.Silu),
                          reads=["cvt"], writes=[f"cvo{bi}"])
                    em.op("sp", lambda e, bi=bi, oc=oc: e.dma_start(
                        out=self.conv_spill[oc * 128:(oc + 1) * 128, :], in_=cvo[bi][:]),
                        reads=[f"cvo{bi}"], writes=["conv_spill"], dma=True)

            if self.debug:
                em.op("pool", lambda e: e.dma_start(out=self.dbg["xfmctx"], in_=self.xfm_ctx[:]),
                      reads=["xfm_ctx"], dma=True)
                em.op("sp", lambda e: e.dma_start(out=self.dbg["conv"], in_=self.conv_spill),
                      reads=["conv_spill"], dma=True)
                em.op("sp", lambda e: e.dma_start(out=self.dbg["dt"], in_=self.dt_hm[:]),
                      reads=["dt_hm"], dma=True)
            em.barrier()

    def phase2(self):
        em, nc = self.em, self.nc
        pb, PB = self.pb, self.PB
        self.p2stack = contextlib.ExitStack()
        ph = self.p2stack
        if True:
            def psb(name, shape, dt=F32):
                return ph.enter_context(nc.sbuf_tensor(name, list(shape), dt))

            xbB = [psb(f"xbB{i}", [128, 16 * 128]) for i in range(2)]
            x_tm = psb("x_tm", [128, 1024])
            b_tm = psb("b_tm", [128, 1024], BF16)
            dA = psb("dA", [64, 128])
            lrow = psb("lrow", [1, 384])
            self.lrow_ref = lrow
            dtmp = psb("dtmp", [128, 3 * 64])
            self.dtmp_ref = dtmp
            xdec = psb("xdec", [128, 512], BF16)
            gS = psb("gS", [64, 4096]); self.gS_ref = gS
            gS2 = psb("gS2", [64, 4096]); self.gS2_ref = gS2
            gl = psb("gl", [1, 64]); self.gl_ref = gl
            fmrow = psb("fmrow", [1, 16]); self.fmrow_ref = fmrow
            maskb = psb("maskb", [64, 8]); self.maskb_ref = maskb

            wz = [psb(f"wz{i}", [128, 512], BF16) for i in range(4)]
            zb = [psb(f"zb{i}", [128, 512]) for i in range(2)]

            def z_step_gen():
                for u in range(16):
                    tcg, zn = divmod(u, 8)
                    for k in range(16):
                        bi = k % 4
                        wk = f"wz{bi}"
                        em.op("pool", lambda e, bi=bi, k=k, zn=zn: e.dma_start(
                            out=wz[bi][:], in_=self.w_in_z[zn, k]),
                            writes=[wk], dma=True)
                        for tt in range(4):
                            tc = tcg * 4 + tt
                            em.op("pe", lambda e, bi=bi, k=k, tt=tt, tc=tc: e.matmul(
                                pb[2 + tt][:, 0:512],
                                self.x_fm[:, k * T + tc * 128:k * T + (tc + 1) * 128],
                                wz[bi][:], start=(k == 0), stop=(k == 15)),
                                reads=[wk, "x_fm"], writes=[PB[2 + tt]],
                                inc=(1 if tt == 3 else 0))
                        if k % 4 == 3 and k < 15:
                            yield
                    for tt in range(4):
                        tc = tcg * 4 + tt
                        zi = tt % 2
                        em.op("act", lambda e, tt=tt, zi=zi: e.activation(
                            zb[zi][:], pb[2 + tt][:, 0:512], AF.Silu),
                            reads=[PB[2 + tt]], writes=[f"zb{zi}"])
                        em.op("sp", lambda e, tc=tc, zn=zn, zi=zi: e.dma_start(
                            out=self.z_spill[tc * 128:(tc + 1) * 128,
                                             zn * 512:(zn + 1) * 512],
                            in_=zb[zi][:]),
                            reads=[f"zb{zi}"], writes=["z_spill"], dma=True)
                    yield

            zgen = z_step_gen() if self.upto >= 3 else iter(())
            zp = [0]

            def z_pump():
                if zp[0] >= 60:
                    return
                zp[0] += 1
                try:
                    next(zgen)
                except StopIteration:
                    pass

            em.op("sp", lambda e: e.dma_start(out=fmrow[:], in_=self.foldmask),
                  writes=["fmrow"], dma=True)
            em.op("pe", lambda e: e.matmul(pb[0][0:64, 0:8], self.cfv("onesrow")[0:1, 0:64],
                                           fmrow[0:1, 0:8], start=True, stop=True),
                  reads=["fmrow", "scf"], writes=[PB[0]])
            em.op("dve", lambda e: e.tensor_copy(maskb[:], pb[0][0:64, 0:8]),
                  reads=[PB[0]], writes=["maskb"])
            em.op("dve", lambda e: e.memset(self.prevs[0][:], 0.0), writes=["prevs0"])
            em.op("dve", lambda e: e.memset(self.sumlast[:], 0.0), writes=["sumlast"])

            for c in range(self.p2_chunks):
                bi = c % 2
                xk = f"xbB{bi}"
                em.op("sp", lambda e, bi=bi, c=c: e.dma_start(
                    out=xbB[bi][:].rearrange("p (kc n) -> p kc n", kc=16),
                    in_=self.conv_spill[0:2048, c * 128:(c + 1) * 128]
                        .rearrange("(kc p) n -> p kc n", p=128)),
                    writes=[xk], dma=True)
                for g8 in range(8 if self.p2_parts & 1 else 0):
                    bk = PB[g8 % 2]
                    em.op("pe", lambda e, g8=g8, bi=bi: e.transpose(
                        pb[g8 % 2][:, 0:128], xbB[bi][:, g8 * 128:(g8 + 1) * 128],
                        self.cfv("identf")),
                        reads=[xk, "scf"], writes=[bk])
                    em.op("dve", lambda e, g8=g8: e.tensor_copy(
                        x_tm[:, g8 * 128:(g8 + 1) * 128], pb[g8 % 2][:, 0:128]),
                        reads=[bk], writes=["x_tm"])
                for g8 in range(8 if self.p2_parts & 1 else 0):
                    bk = PB[g8 % 2]
                    em.op("pe", lambda e, g8=g8, bi=bi: e.transpose(
                        pb[g8 % 2][:, 0:128], xbB[bi][:, (8 + g8) * 128:(9 + g8) * 128],
                        self.cfv("identf")),
                        reads=[xk, "scf"], writes=[bk])
                    em.op("dve", lambda e, g8=g8: e.tensor_copy(
                        b_tm[:, g8 * 128:(g8 + 1) * 128], pb[g8 % 2][:, 0:128]),
                        reads=[bk], writes=["b_tm"])
                if not (self.p2_parts & 2):
                    continue
                _n = [0]
                _lim = getattr(self, "p2_fine", 99)
                _orig_op = em.op
                def _gated_op(*a, **k):
                    if _n[0] >= _lim:
                        _n[0] += 1
                        return
                    _n[0] += 1
                    return _orig_op(*a, **k)
                em.op = _gated_op
                em.op("dve", lambda e, c=c: e.tensor_scalar(
                    dA[:], self.dt_hm[:, c * 128:(c + 1) * 128], self.acol[0:64, 1:2],
                    None, ALU.mult),
                    reads=["dt_hm", "acol"], writes=["dA"])
                em.op("dve", lambda e, c=c: e.tensor_tensor_scan(
                    self.dAc_hm[:, c * 128:(c + 1) * 128], dA[:], dA[:], 0.0,
                    ALU.add, ALU.bypass),
                    reads=["dA"], writes=["dAc_hm"])
                em.op("pe", lambda e, c=c: e.transpose(
                    pb[0][:, 0:64], self.dAc_hm[0:64, c * 128:(c + 1) * 128],
                    self.cfv("identf")[0:64, 0:64]),
                    reads=["dAc_hm", "scf"], writes=[PB[0]])
                em.op("dve", lambda e, c=c: e.tensor_copy(
                    self.dAcT[:, c * 64:(c + 1) * 64], pb[0][:, 0:64]),
                    reads=[PB[0]], writes=["dAcT"])
                em.op("pe", lambda e, c=c: e.transpose(
                    pb[1][:, 0:64], self.dt_hm[0:64, c * 128:(c + 1) * 128],
                    self.cfv("identf")[0:64, 0:64]),
                    reads=["dt_hm", "scf"], writes=[PB[1]])
                em.op("dve", lambda e, c=c: e.tensor_copy(
                    self.dtT[:, c * 64:(c + 1) * 64], pb[1][:, 0:64]),
                    reads=[PB[1]], writes=["dtT"])
                em.op("dve", lambda e, c=c: e.tensor_scalar(
                    self.ndAcT[:, c * 64:(c + 1) * 64], self.dAcT[:, c * 64:(c + 1) * 64],
                    -1.0, None, ALU.mult),
                    reads=["dAcT"], writes=["ndAcT"])
                em.op("act", lambda e, c=c: e.activation(
                    self.edAcT[:, c * 64:(c + 1) * 64], self.dAcT[:, c * 64:(c + 1) * 64],
                    AF.Exp),
                    reads=["dAcT"], writes=["edAcT"])
                em.op("pe", lambda e, c=c: e.transpose(
                    pb[0][0:1, 0:64], self.dAc_hm[0:64, c * 128 + 127:c * 128 + 128],
                    self.cfv("identf")[0:64, 0:64]),
                    reads=["dAc_hm", "scf"], writes=[PB[0]])
                lo = (c % 2) * 192
                em.op("dve", lambda e, lo=lo: e.tensor_copy(lrow[0:1, lo:lo + 64], pb[0][0:1, 0:64]),
                      reads=[PB[0]], writes=["lrow"])
                em.op("dve", lambda e, lo=lo: e.tensor_add(
                    self.sumlast[0:1, 0:64], self.sumlast[0:1, 0:64], lrow[0:1, lo:lo + 64]),
                    reads=["lrow", "sumlast"], writes=["sumlast"])
                em.op("act", lambda e, lo=lo: e.activation(
                    lrow[0:1, lo + 64:lo + 128], lrow[0:1, lo:lo + 64], AF.Exp),
                    reads=["lrow"], writes=["lrow"])
                em.op("pe", lambda e, lo=lo: e.matmul(
                    pb[0][0:64, 0:64], self.cfv("onesrow")[0:1, 0:64],
                    lrow[0:1, lo + 64:lo + 128],
                    start=True, stop=True),
                    reads=["lrow", "scf"], writes=[PB[0]])
                em.op("dve", lambda e, c=c: e.tensor_copy(
                    self.cdb[:, c * 64:(c + 1) * 64], pb[0][0:64, 0:64]),
                    reads=[PB[0]], writes=["cdb"])
                em.op("pe", lambda e, lo=lo: e.matmul(
                    pb[1][:, 0:64], self.cfv("onesrow")[0:1, :], lrow[0:1, lo:lo + 64],
                    start=True, stop=True),
                    reads=["lrow", "scf"], writes=[PB[1]])
                em.op("dve", lambda e, c=c: e.tensor_sub(
                    dtmp[:, 0:64], pb[1][:, 0:64], self.dAcT[:, c * 64:(c + 1) * 64]),
                    reads=[PB[1], "dAcT"], writes=["dtmp"])
                em.op("act", lambda e: e.activation(dtmp[:, 64:128], dtmp[:, 0:64], AF.Exp),
                      reads=["dtmp"], writes=["dtmp"])
                em.op("dve", lambda e, c=c: e.tensor_mul(
                    self.decT[:, c * 64:(c + 1) * 64], self.dtT[:, c * 64:(c + 1) * 64],
                    dtmp[:, 64:128]),
                    reads=["dtmp", "dtT"], writes=["decT"])
                em.op = _orig_op
                z_pump()
                src = c % 2
                dst = 1 - src
                for w in range(8 if self.p2_parts & 4 else 0):
                    for hh in range(8):
                        h = w * 8 + hh
                        g = h // 4
                        em.op("dve", lambda e, h=h, hh=hh, g=g, c=c: e.tensor_scalar(
                            xdec[:, hh * 64:(hh + 1) * 64], x_tm[:, g * 64:(g + 1) * 64],
                            self.decT[:, c * 64 + h:c * 64 + h + 1], None, ALU.mult),
                            reads=["x_tm", "decT"], writes=["xdec"])
                    for hh in range(8):
                        h = w * 8 + hh
                        g = h // 4
                        em.op("pe", lambda e, h=h, hh=hh, g=g: e.matmul(
                            pb[0][0:64, hh * 64:(hh + 1) * 64],
                            b_tm[:, g * 64:(g + 1) * 64], xdec[:, hh * 64:(hh + 1) * 64],
                            start=True, stop=True),
                            reads=["b_tm", "xdec"], writes=[PB[0]],
                            inc=(1 if hh == 7 else 0))
                    for hh in range(8):
                        h = w * 8 + hh
                        em.op("dve", lambda e, h=h, hh=hh, c=c, src=src, dst=dst:
                              e.scalar_tensor_tensor(
                                  self.prevs[dst][:, h * 64:(h + 1) * 64],
                                  self.prevs[src][:, h * 64:(h + 1) * 64],
                                  self.cdb[0:64, c * 64 + h:c * 64 + h + 1],
                                  pb[0][0:64, hh * 64:(hh + 1) * 64],
                                  ALU.mult, ALU.add),
                              reads=[f"prevs{src}", "cdb", PB[0]],
                              writes=[f"prevs{dst}"])
                    z_pump()
            if not self.p2_tail:
                em.barrier()
                return
            final = NCH % 2
            em.op("pool", lambda e, final=final: e.dma_start(
                out=self.cc_in.ap()[0:64, :], in_=self.prevs[final][:]),
                reads=[f"prevs{final}"], writes=["cc_in"], dma=True)
            em.op("pool", lambda e: e.dma_start(
                out=self.cc_in.ap()[64:65, :].rearrange("one (p n) -> one p n", p=128),
                in_=self.cfv("zeros32")),
                reads=["scf"], writes=["cc_in"], dma=True)
            em.op("pool", lambda e: e.dma_start(
                out=self.cc_in.ap()[64:65, 0:64], in_=self.sumlast[0:1, 0:64]),
                reads=["sumlast"], writes=["cc_in"], dma=True)
            if self.n_cores > 1 and not getattr(self, "no_collective", False):
                em.op("pool", lambda e: e.collective_compute(
                    "AllGather", ALU.bypass, replica_groups=[list(range(self.n_cores))],
                    ins=[self.cc_in.ap().opt()], outs=[self.cc_out.ap().opt()]),
                    reads=["cc_in"], writes=["cc_out"], cc=True)
            else:
                em.op("pool", lambda e: e.dma_start(out=self.cc_out.ap()[0:65, :],
                                                    in_=self.cc_in.ap()),
                      reads=["cc_in"], writes=["cc_out"], dma=True)
            for _ in zgen:
                pass
            if self.upto >= 3 and self.debug:
                em.op("sp", lambda e: e.dma_start(out=self.dbg["z"], in_=self.z_spill),
                      reads=["z_spill"], dma=True)
            em.barrier(skip_cc=True)

    def phase2fold(self):
        em, nc = self.em, self.nc
        pb, PB = self.pb, self.PB
        gS_ = self.gS_ref; gS2_ = self.gS2_ref; gl_ = self.gl_ref
        fmrow_ = self.fmrow_ref; maskb_ = self.maskb_ref
        dtmp_ = self.dtmp_ref; lrow_ = self.lrow_ref
        if True:
            em.op("dve", lambda e: e.memset(self.prevs[0][:], 0.0),
                  writes=["prevs0"])
            em.op("dve", lambda e: e.memset(self.prevs[1][:], 0.0),
                  writes=["prevs1"])
            nfold = (min(7, self.n_cores - 1) if self.n_cores > 1 else 0)
            if getattr(self, "skip_fold", False):
                nfold = 0
            for m in range(nfold):
                src, dst = m % 2, 1 - m % 2
                em.op("pool", lambda e, m=m: e.dma_start(
                    out=gS_[:], in_=self.cc_out.ap()[m * 65:m * 65 + 64, :]),
                    reads=["cc_out"], writes=["gS"], dma=True)
                em.op("pool", lambda e, m=m: e.dma_start(
                    out=gl_[:], in_=self.cc_out.ap()[m * 65 + 64:m * 65 + 65, 0:64]),
                    reads=["cc_out"], writes=["gl"], dma=True)
                lo = (m % 2) * 192
                em.op("act", lambda e, lo=lo: e.activation(
                    lrow_[0:1, lo:lo + 64], gl_[0:1, :], AF.Exp),
                    reads=["gl"], writes=["lrow"])
                em.op("dve", lambda e, m=m, lo=lo: e.tensor_scalar(
                    lrow_[0:1, lo + 64:lo + 128], lrow_[0:1, lo:lo + 64], fmrow_[0:1, m:m + 1],
                    fmrow_[0:1, 8 + m:9 + m], ALU.mult, ALU.add),
                    reads=["lrow", "fmrow"], writes=["lrow"])
                em.op("pe", lambda e, lo=lo: e.matmul(
                    pb[4][0:64, 0:64], self.cfv("onesrow")[0:1, 0:64],
                    lrow_[0:1, lo + 64:lo + 128],
                    start=True, stop=True),
                    reads=["lrow", "scf"], writes=[PB[4]])
                em.op("dve", lambda e: e.tensor_copy(dtmp_[0:64, 128:192], pb[4][0:64, 0:64]),
                      reads=[PB[4]], writes=["dtmp"])
                em.op("dve", lambda e, m=m: e.tensor_scalar(
                    gS2_[:], gS_[:], maskb_[:, m:m + 1], None, ALU.mult),
                    reads=["gS", "maskb"], writes=["gS2"])
                for h in range(NH):
                    em.op("dve", lambda e, h=h, src=src, dst=dst: e.scalar_tensor_tensor(
                        self.prevs[dst][:, h * 64:(h + 1) * 64],
                        self.prevs[src][:, h * 64:(h + 1) * 64],
                        dtmp_[0:64, 128 + h:129 + h],
                        gS2_[:, h * 64:(h + 1) * 64],
                        ALU.mult, ALU.add),
                        reads=[f"prevs{src}", "dtmp", "gS2"],
                        writes=[f"prevs{dst}"])
            if self.debug:
                em.op("pool", lambda e: e.dma_start(
                    out=self.dbg["prevs"], in_=self.prevs[1][:]),
                    reads=["prevs1"], dma=True)
                em.op("pool", lambda e: e.dma_start(out=self.dbg["dacT"], in_=self.dAcT[:]),
                      reads=["dAcT"], dma=True)
                em.op("pool", lambda e: e.dma_start(out=self.dbg["edacT"], in_=self.edAcT[:]),
                      reads=["edAcT"], dma=True)
                em.op("pool", lambda e: e.dma_start(out=self.dbg["dachm"], in_=self.dAc_hm[:]),
                      reads=["dAc_hm"], dma=True)
            em.barrier()
        self.p2stack.close()

    def phase2z(self):
        em, nc = self.em, self.nc
        pb, PB = self.pb, self.PB
        with contextlib.ExitStack() as ph:
            def psb(name, shape, dt=F32):
                return ph.enter_context(nc.sbuf_tensor(name, list(shape), dt))

            wz = [psb(f"wz{i}", [128, 512], BF16) for i in range(2)]
            zb = [psb(f"zb{i}", [128, 512]) for i in range(2)]
            for tcg in range(2):
                for zn in range(8):
                    for k in range(16):
                        bi = k % 2
                        wk = f"wz{bi}"
                        em.op("pool", lambda e, bi=bi, k=k, zn=zn: e.dma_start(
                            out=wz[bi][:], in_=self.w_in_z[zn, k]),
                            writes=[wk], dma=True)
                        for tt in range(4):
                            tc = tcg * 4 + tt
                            em.op("pe", lambda e, bi=bi, k=k, tt=tt, tc=tc: e.matmul(
                                pb[tt][:, 0:512],
                                self.x_fm[:, k * T + tc * 128:k * T + (tc + 1) * 128],
                                wz[bi][:], start=(k == 0), stop=(k == 15)),
                                reads=[wk, "x_fm"], writes=[PB[tt]],
                                inc=(1 if tt == 3 else 0))
                    for tt in range(4):
                        tc = tcg * 4 + tt
                        zi = tt % 2
                        em.op("act", lambda e, tt=tt, zi=zi: e.activation(
                            zb[zi][:], pb[tt][:, 0:512], AF.Silu),
                            reads=[PB[tt]], writes=[f"zb{zi}"])
                        em.op("sp", lambda e, tc=tc, zn=zn, zi=zi: e.dma_start(
                            out=self.z_spill[tc * 128:(tc + 1) * 128,
                                             zn * 512:(zn + 1) * 512],
                            in_=zb[zi][:]),
                            reads=[f"zb{zi}"], writes=["z_spill"], dma=True)
            if self.debug:
                em.op("sp", lambda e: e.dma_start(out=self.dbg["z"], in_=self.z_spill),
                      reads=["z_spill"], dma=True)

    def phase3(self):
        em, nc = self.em, self.nc
        pb, pt, PB, PT = self.pb, self.pt, self.PB, self.PT
        self.xstack.close()
        with contextlib.ExitStack() as ph:
            def psb(name, shape, dt=F32):
                return ph.enter_context(nc.sbuf_tensor(name, list(shape), dt))

            bcf = psb("bcf", [64, 80 * 128], BF16)
            xbr = [psb(f"xbr{i}", [128, 8 * 128]) for i in range(2)]
            x_tm = psb("x_tm3", [128, 1024])
            b_tm = psb("b_tm3", [128, 1024], BF16)
            Lw = psb("Lw", [128, 1024])
            S1 = psb("S1", [128, 1024], BF16)
            xdt = psb("xdt", [128, 512], BF16)
            xdec = psb("xdec3", [128, 512], BF16)
            pcb = psb("pcb", [64, 4096], BF16)
            twave = psb("twave", [128, 512])
            uwave = psb("uwave", [128, 512])
            ygsl = psb("ygsl", [128, 512])
            siluz = [psb(f"siluz{i}", [128, 512]) for i in range(2)]
            ygs = psb("ygs", [128, 4096])
            ybf = psb("ybf", [128, 1024], BF16)
            yfm = psb("yfm", [128, 32 * 128], BF16)
            ncol = psb("ncol", [128, 10])
            wo32 = [psb(f"wo32_{i}", [128, 512]) for i in range(2)]
            wobf = [psb(f"wobf{i}", [128, 512], BF16) for i in range(2)]
            yk = [psb(f"yk{i}", [128, 512], BF16) for i in range(2)]
            a0r = [psb(f"a0r{i}", [128, 512]) for i in range(2)]
            pr1 = psb("pr1", [128, 512])
            r1sb = [psb(f"r1sb{i}", [128, 512]) for i in range(2)]

            em.op("sp", lambda e: e.dma_start(out=self.normw_sb[:], in_=self.norm_w),
                  writes=["normw_sb"], dma=True)

            for c in range(self.p3_chunks):
                bi = c % 2
                bck = "bcf"
                xbk = f"xbr{bi}"
                em.op("pool", lambda e, c=c: e.dma_start(
                    out=bcf[:].rearrange("p (kc n) -> p kc n", kc=80),
                    in_=self.conv_spill[1024:6144, c * 128:(c + 1) * 128]
                        .rearrange("(kc pl) n -> pl kc n", pl=64)),
                    writes=[bck], dma=True)
                em.op("sp", lambda e, bi=bi, c=c: e.dma_start(
                    out=xbr[bi][:].rearrange("p (kc n) -> p kc n", kc=8),
                    in_=self.conv_spill[0:1024, c * 128:(c + 1) * 128]
                        .rearrange("(kc p) n -> p kc n", p=128)),
                    writes=[xbk], dma=True)

                for g8 in range(8):
                    bk = PB[g8 % 2]
                    em.op("pe", lambda e, g8=g8, bi=bi: e.transpose(
                        pb[g8 % 2][:, 0:128], xbr[bi][:, g8 * 128:(g8 + 1) * 128],
                        self.cfv("identf")),
                        reads=[xbk, "scf"], writes=[bk])
                    em.op("dve", lambda e, g8=g8: e.tensor_copy(
                        x_tm[:, g8 * 128:(g8 + 1) * 128], pb[g8 % 2][:, 0:128]),
                        reads=[bk], writes=["x_tm3"])
                for g in range(16):
                    bk = PT[g % 2]
                    em.op("pe", lambda e, g=g: e.transpose(
                        pt[g % 2][:, 0:64], bcf[0:64, g * 128:(g + 1) * 128],
                        self.cbv("identb")[0:64, 0:64]),
                        reads=[bck, "scb"], writes=[bk])
                    em.op("dve", lambda e, g=g: e.tensor_copy(
                        b_tm[:, g * 64:(g + 1) * 64], pt[g % 2][:, 0:64]),
                        reads=[bk], writes=["b_tm3"])

                def cview(h):
                    return bcf[0:64, (16 + h) * 128:(17 + h) * 128]

                def bviewm(h):
                    g = h // 4
                    return bcf[0:64, g * 128:(g + 1) * 128]

                src = (c + 1) % 2
                dst = 1 - src
                em.op("dve", lambda e, src=src: e.tensor_copy(
                    pcb[:], self.prevs[src][:]),
                    reads=[f"prevs{src}"], writes=["pcb"])
                for w in range(8 if self.p3_parts & 1 else 0):
                    h0 = w * 8
                    em.op("sp", lambda e, c=c, w=w: e.dma_start(
                        out=siluz[w % 2][:],
                        in_=self.z_spill[c * 128:(c + 1) * 128, w * 512:(w + 1) * 512]),
                        writes=[f"siluz{w % 2}"], dma=True)
                    for hh in range(8):
                        h = h0 + hh
                        bk = PB[hh // 4]
                        em.op("pe", lambda e, h=h, hh=hh, c=c: e.matmul(
                            pb[hh // 4][:, (hh % 4) * 128:(hh % 4 + 1) * 128],
                            self.selv(h), self.dAc_hm[0:64, c * 128:(c + 1) * 128],
                            start=True, stop=False),
                            reads=["ssel", "dAc_hm"], writes=[bk], inc=0)
                        em.op("pe", lambda e, hh=hh: e.matmul(
                            pb[hh // 4][:, (hh % 4) * 128:(hh % 4 + 1) * 128],
                            self.cfv("identf"), self.cfv("trilneg"),
                            start=False, stop=True),
                            reads=["scf"], writes=[bk],
                            inc=(1 if hh % 4 == 3 else 0))
                    for hh in range(8):
                        h = h0 + hh
                        em.op("act", lambda e, h=h, hh=hh, c=c: e.activation(
                            Lw[:, hh * 128:(hh + 1) * 128],
                            pb[hh // 4][:, (hh % 4) * 128:(hh % 4 + 1) * 128],
                            AF.Exp, bias=self.ndAcT[:, c * 64 + h:c * 64 + h + 1]),
                            reads=[PB[hh // 4], "ndAcT"], writes=["Lw"])
                    for hh in range(8 if self.p3_parts & 2 else 0):
                        h = h0 + hh
                        g = h // 4
                        bk = PB[2 + hh // 4]
                        em.op("pe", lambda e, h=h, hh=hh, cview=cview, bviewm=bviewm:
                              e.matmul(
                                  pb[2 + hh // 4][:, (hh % 4) * 128:(hh % 4 + 1) * 128],
                                  bviewm(h), cview(h), start=True, stop=True),
                              reads=[bck], writes=[bk],
                              inc=(1 if hh % 4 == 3 else 0))
                    for half in range(2 if self.p3_parts & 256 else 0):
                        em.op("dve", lambda e, half=half: e.tensor_mul(
                            S1[:, half * 512:(half + 1) * 512],
                            Lw[:, half * 512:(half + 1) * 512],
                            pb[2 + half][:, 0:512]),
                            reads=["Lw", PB[2 + half]], writes=["S1"])
                    for hh in range(8 if self.p3_parts & 4 else 0):
                        h = h0 + hh
                        g = h // 4
                        em.op("dve", lambda e, h=h, hh=hh, g=g, c=c: e.tensor_scalar(
                            xdt[:, hh * 64:(hh + 1) * 64], x_tm[:, g * 64:(g + 1) * 64],
                            self.dtT[:, c * 64 + h:c * 64 + h + 1], None, ALU.mult),
                            reads=["x_tm3", "dtT"], writes=["xdt"])
                    for hh in range(8 if self.p3_parts & 4 else 0):
                        em.op("pe", lambda e, hh=hh: e.matmul(
                            pb[4][:, hh * 64:(hh + 1) * 64],
                            S1[:, hh * 128:(hh + 1) * 128], xdt[:, hh * 64:(hh + 1) * 64],
                            start=True, stop=True),
                            reads=["S1", "xdt"], writes=[PB[4]],
                            inc=(1 if hh == 7 else 0))
                    for hh in range(8 if self.p3_parts & 8 else 0):
                        h = h0 + hh
                        em.op("pe", lambda e, h=h, hh=hh, cview=cview: e.matmul(
                            pb[5][:, hh * 64:(hh + 1) * 64],
                            cview(h), pcb[0:64, h * 64:(h + 1) * 64],
                            start=True, stop=True),
                            reads=[bck, "pcb"], writes=[PB[5]],
                            inc=(1 if hh == 7 else 0))
                    for hh in range(8 if self.p3_parts & 16 else 0):
                        h = h0 + hh
                        g = h // 4
                        em.op("dve", lambda e, h=h, hh=hh, g=g, c=c: e.tensor_scalar(
                            xdec[:, hh * 64:(hh + 1) * 64], x_tm[:, g * 64:(g + 1) * 64],
                            self.decT[:, c * 64 + h:c * 64 + h + 1], None, ALU.mult),
                            reads=["x_tm3", "decT"], writes=["xdec3"])
                    for hh in range(8 if self.p3_parts & 16 else 0):
                        h = h0 + hh
                        g = h // 4
                        em.op("pe", lambda e, h=h, hh=hh, g=g: e.matmul(
                            pb[0][0:64, hh * 64:(hh + 1) * 64],
                            b_tm[:, g * 64:(g + 1) * 64], xdec[:, hh * 64:(hh + 1) * 64],
                            start=True, stop=True),
                            reads=["b_tm3", "xdec3"], writes=[PB[0]],
                            inc=(1 if hh == 7 else 0))
                    for hh in range(8 if self.p3_parts & 16 else 0):
                        h = h0 + hh
                        em.op("dve", lambda e, h=h, hh=hh, c=c, src=src, dst=dst:
                              e.scalar_tensor_tensor(
                                  self.prevs[dst][:, h * 64:(h + 1) * 64],
                                  self.prevs[src][:, h * 64:(h + 1) * 64],
                                  self.cdb[0:64, c * 64 + h:c * 64 + h + 1],
                                  pb[0][0:64, hh * 64:(hh + 1) * 64],
                                  ALU.mult, ALU.add),
                              reads=[f"prevs{src}", "cdb", PB[0]],
                              writes=[f"prevs{dst}"])
                    for hh in range(8 if self.p3_parts & 32 else 0):
                        h = h0 + hh
                        em.op("dve", lambda e, h=h, hh=hh, c=c: e.tensor_scalar(
                            twave[:, hh * 64:(hh + 1) * 64],
                            pb[5][:, hh * 64:(hh + 1) * 64],
                            self.edAcT[:, c * 64 + h:c * 64 + h + 1], None, ALU.mult),
                            reads=[PB[5], "edAcT"], writes=["twave"])
                    for hh in range(8 if self.p3_parts & 32 else 0):
                        h = h0 + hh
                        g = h // 4
                        em.op("dve", lambda e, h=h, hh=hh, g=g: e.scalar_tensor_tensor(
                            uwave[:, hh * 64:(hh + 1) * 64], x_tm[:, g * 64:(g + 1) * 64],
                            self.dbcast[:, h:h + 1], twave[:, hh * 64:(hh + 1) * 64],
                            ALU.mult, ALU.add),
                            reads=["x_tm3", "dbcast", "twave"], writes=["uwave"])
                    em.op("dve", lambda e, h0=h0: e.tensor_add(
                        ygsl[:], uwave[:], pb[4][:, 0:512]),
                        reads=["uwave", PB[4]], writes=["ygsl"])
                    em.op("dve", lambda e, h0=h0, w=w: e.tensor_mul(
                        ygs[:, h0 * 64:(h0 + 8) * 64], ygsl[:], siluz[w % 2][:]),
                        reads=["ygsl", f"siluz{w % 2}"], writes=["ygs"])

                if not (self.p3_parts & 64):
                    continue
                em.op("act", lambda e: e.activation(
                    Lw[:], ygs[:, 0:1024], AF.Square, accum_out=ncol[:, 0:1]),
                    reads=["ygs"], writes=["Lw", "ncol"])
                em.op("act", lambda e: e.activation(
                    Lw[:], ygs[:, 1024:2048], AF.Square, accum_out=ncol[:, 1:2]),
                    reads=["ygs"], writes=["Lw", "ncol"])
                em.op("act", lambda e: e.activation(
                    Lw[:], ygs[:, 2048:3072], AF.Square, accum_out=ncol[:, 2:3]),
                    reads=["ygs"], writes=["Lw", "ncol"])
                em.op("act", lambda e: e.activation(
                    Lw[:], ygs[:, 3072:4096], AF.Square, accum_out=ncol[:, 3:4]),
                    reads=["ygs"], writes=["Lw", "ncol"])
                em.op("dve", lambda e: e.tensor_add(
                    ncol[:, 4:5], ncol[:, 0:1], ncol[:, 1:2]), reads=["ncol"], writes=["ncol2"])
                em.op("dve", lambda e: e.tensor_add(
                    ncol[:, 5:6], ncol[:, 2:3], ncol[:, 3:4]), reads=["ncol"], writes=["ncol2"])
                em.op("dve", lambda e: e.tensor_add(
                    ncol[:, 6:7], ncol[:, 4:5], ncol[:, 5:6]),
                    reads=["ncol2"], writes=["ncol3"])
                em.op("dve", lambda e: e.tensor_scalar(
                    ncol[:, 7:8], ncol[:, 6:7], 1.0 / DI, None, ALU.mult),
                    reads=["ncol3"], writes=["ncol4"])
                em.op("act", lambda e: e.activation(
                    ncol[:, 8:9], ncol[:, 7:8], AF.Sqrt, bias=self.cfv("epscol")),
                    reads=["ncol4", "scf"], writes=["ncol5"])
                em.op("dve", lambda e: e.reciprocal(ncol[:, 9:10], ncol[:, 8:9]),
                      reads=["ncol5"], writes=["ncol6"])
                for q4 in range(4):
                    em.op("dve", lambda e, q4=q4: e.tensor_scalar(
                        ybf[:], ygs[:, q4 * 1024:(q4 + 1) * 1024], ncol[:, 9:10],
                        None, ALU.mult),
                        reads=["ygs", "ncol6"], writes=["ybf"])
                    for f8 in range(8):
                        fc = q4 * 8 + f8
                        bk = PT[f8 % 2]
                        em.op("pe", lambda e, f8=f8: e.transpose(
                            pt[f8 % 2][:, 0:128], ybf[:, f8 * 128:(f8 + 1) * 128],
                            self.cbv("identb")),
                            reads=["ybf", "scb"], writes=[bk])
                        em.op("dve", lambda e, fc=fc, f8=f8: e.tensor_copy(
                            yfm[:, fc * 128:(fc + 1) * 128], pt[f8 % 2][:, 0:128]),
                            reads=[bk], writes=["yfm"])
                em.op("sp", lambda e, c=c: e.dma_start(
                    out=self.yfm_spill[0:4096, c * 128:(c + 1) * 128]
                        .rearrange("(kc p) n -> p kc n", p=128),
                    in_=yfm[:].rearrange("p (kc n) -> p kc n", kc=32)),
                    reads=["yfm"], writes=["yfm_spill"], dma=True)

                if c in (3, 7) and (self.p3_parts & 128):
                    blk = c // 4
                    for on in range(4):
                        for k in range(32):
                            wi = k % 2
                            em.op("sp", lambda e, wi=wi, k=k, on=on: e.dma_start(
                                out=wo32[wi][:], in_=self.w_out_t[on, k]),
                                writes=[f"wo32_{wi}"], dma=True)
                            em.op("dve", lambda e, wi=wi, k=k: e.tensor_scalar(
                                wobf[wi][:], wo32[wi][:], self.normw_sb[:, k:k + 1],
                                None, ALU.mult),
                                reads=[f"wo32_{wi}", "normw_sb"], writes=[f"wobf{wi}"])
                            em.op("pool", lambda e, wi=wi, k=k, blk=blk: e.dma_start(
                                out=yk[wi][:], in_=self.yfm_spill[k * 128:(k + 1) * 128,
                                                                  blk * 512:(blk + 1) * 512]),
                                reads=["yfm_spill"], writes=[f"yk{wi}"], dma=True)
                            for tt in range(4):
                                em.op("pe", lambda e, wi=wi, k=k, tt=tt: e.matmul(
                                    pb[tt][:, 0:512], yk[wi][:, tt * 128:(tt + 1) * 128],
                                    wobf[wi][:], start=(k == 0), stop=(k == 31)),
                                    reads=[f"yk{wi}", f"wobf{wi}"], writes=[PB[tt]],
                                    inc=(1 if tt == 3 else 0))
                        for tt in range(4):
                            cg = blk * 4 + tt
                            ai = tt % 2
                            em.op("sp", lambda e, cg=cg, on=on, ai=ai: e.dma_start(
                                out=a0r[ai][:], in_=self.x_own[cg * 128:(cg + 1) * 128,
                                                               on * 512:(on + 1) * 512]),
                                writes=[f"a0r{ai}"], dma=True)
                            em.op("dve", lambda e, cg=cg, ai=ai: e.tensor_scalar(
                                pr1[:], a0r[ai][:], self.stat(cg, 0), self.stat(cg, 1),
                                ALU.subtract, ALU.mult),
                                reads=[f"a0r{ai}", "stats"], writes=["pr1"])
                            ri = tt % 2
                            em.op("dve", lambda e, tt=tt, ri=ri: e.tensor_add(
                                r1sb[ri][:], pr1[:], pb[tt][:, 0:512]),
                                reads=["pr1", PB[tt]], writes=[f"r1sb{ri}"])
                            em.op("sp", lambda e, cg=cg, on=on, ri=ri: e.dma_start(
                                out=self.r1_spill[cg * 128:(cg + 1) * 128,
                                                  on * 512:(on + 1) * 512],
                                in_=r1sb[ri][:]),
                                reads=[f"r1sb{ri}"], writes=["r1_spill"], dma=True)

            if self.debug:
                em.op("sp", lambda e: e.dma_start(out=self.dbg["r1"], in_=self.r1_spill),
                      reads=["r1_spill"], dma=True)
                em.op("sp", lambda e: e.dma_start(out=self.dbg["y"], in_=self.yfm_spill),
                      reads=["yfm_spill"], dma=True)
            em.barrier()


    def phase5(self):
        em, nc = self.em, self.nc
        pb, pt, PB, PT = self.pb, self.pt, self.PB, self.PT
        self.ssdstack.close()
        with contextlib.ExitStack() as ph:
            def psb(name, shape, dt=F32):
                return ph.enter_context(nc.sbuf_tensor(name, list(shape), dt))

            hm = psb("hm", [128, 64 * 512], BF16)
            h2fm = psb("h2fm", [128, 16 * 512], BF16)
            r1r = [psb(f"r1r{i}", [128, DM]) for i in range(2)]
            h2b = psb("h2b", [128, DM], BF16)
            h2f = psb("h2f", [128, DM])
            sqs5 = psb("sqs5", [128, 1024])
            scc5 = psb("scc5", [128, 8])
            wgb = [psb(f"wgb{i}", [128, 16 * 128], BF16) for i in range(3)]
            wub = [psb(f"wub{i}", [128, 16 * 128], BF16) for i in range(3)]
            wdb = [psb(f"wdb{i}", [128, 1024], BF16) for i in range(4)]
            sgb = psb("sgb", [128, 512])
            a0r5 = [psb(f"a0r5{i}", [128, 512]) for i in range(4)]
            r1s = [psb(f"r1s{i}", [128, 512]) for i in range(4)]
            a1s = [psb(f"a1s{i}", [128, 512]) for i in range(2)]
            t5 = psb("t5", [128, 512])

            for blk in range(2):
                for tc in range(4):
                    cg = blk * 4 + tc
                    ri = tc % 2
                    em.op("sp", lambda e, cg=cg, ri=ri: e.dma_start(
                        out=r1r[ri][:], in_=self.r1_spill[cg * 128:(cg + 1) * 128, :]),
                        writes=[f"r1r{ri}"], dma=True)
                    sq = scc5[:, 0:1]; ms = scc5[:, 1:2]; sqr = scc5[:, 2:3]; rst = scc5[:, 3:4]
                    em.op("act", lambda e, ri=ri, sq=sq: e.activation(
                        sqs5[:], r1r[ri][:, 0:1024], AF.Square, accum_out=sq),
                        reads=[f"r1r{ri}"], writes=["sqs5", "scc5a"])
                    em.op("act", lambda e, ri=ri: e.activation(
                        sqs5[:], r1r[ri][:, 1024:2048], AF.Square, accum_out=scc5[:, 4:5]),
                        reads=[f"r1r{ri}"], writes=["sqs5", "scc5b"])
                    em.op("dve", lambda e, sq=sq, ms=ms: e.tensor_add(ms, sq, scc5[:, 4:5]),
                          reads=["scc5a", "scc5b"], writes=["scc5c"])
                    em.op("dve", lambda e, ms=ms, sqr=sqr: e.tensor_scalar(
                        sqr, ms, 1.0 / DM, None, ALU.mult), reads=["scc5c"], writes=["scc5d"])
                    em.op("act", lambda e, sqr=sqr: e.activation(
                        scc5[:, 5:6], sqr, AF.Sqrt, bias=self.cfv("epscol")),
                        reads=["scc5d", "scf"], writes=["scc5e"])
                    em.op("dve", lambda e, rst=rst: e.reciprocal(rst, scc5[:, 5:6]),
                          reads=["scc5e"], writes=["scc5f"])
                    em.op("dve", lambda e, ri=ri, rst=rst: e.tensor_scalar(
                        h2f[:], r1r[ri][:], rst, None, ALU.mult),
                        reads=[f"r1r{ri}", "scc5f"], writes=["h2f"])
                    em.op("dve", lambda e: e.tensor_mul(h2b[:], h2f[:], self.lnw_b[:, 0:DM]),
                          reads=["h2f", "lnw_b"], writes=["h2b"])
                    for fc in range(16):
                        bk = PT[fc % 2]
                        em.op("pe", lambda e, fc=fc: e.transpose(
                            pt[fc % 2][:, 0:128], h2b[:, fc * 128:(fc + 1) * 128],
                            self.cbv("identb")),
                            reads=["h2b", "scb"], writes=[bk])
                        em.op("dve", lambda e, fc=fc, tc=tc: e.tensor_copy(
                            h2fm[:, fc * 512 + tc * 128: fc * 512 + (tc + 1) * 128],
                            pt[fc % 2][:, 0:128]),
                            reads=[bk], writes=["h2fm"])
                for oc in range(64):
                    wi = oc % 3
                    gk, uk = f"wgb{wi}", f"wub{wi}"
                    em.op("pool", lambda e, wi=wi, oc=oc: e.dma_start(
                        out=wgb[wi][:], in_=self.w_gate_t[oc]),
                        writes=[gk], dma=True)
                    em.op("pool", lambda e, wi=wi, oc=oc: e.dma_start(
                        out=wub[wi][:], in_=self.w_up_t[oc]),
                        writes=[uk], dma=True)
                    bg, bu = 2 * wi, 2 * wi + 1
                    for k in range(16):
                        em.op("pe", lambda e, wi=wi, k=k, bg=bg: e.matmul(
                            pb[bg][:, 0:512], wgb[wi][:, k * 128:(k + 1) * 128],
                            h2fm[:, k * 512:(k + 1) * 512],
                            start=(k == 0), stop=(k == 15)),
                            reads=[gk, "h2fm"], writes=[PB[bg]],
                            inc=(1 if k == 15 else 0))
                        em.op("pe", lambda e, wi=wi, k=k, bu=bu: e.matmul(
                            pb[bu][:, 0:512], wub[wi][:, k * 128:(k + 1) * 128],
                            h2fm[:, k * 512:(k + 1) * 512],
                            start=(k == 0), stop=(k == 15)),
                            reads=[uk, "h2fm"], writes=[PB[bu]],
                            inc=(1 if k == 15 else 0))
                    em.op("act", lambda e, bg=bg: e.activation(sgb[:], pb[bg][:, 0:512], AF.Silu),
                          reads=[PB[bg]], writes=["sgb"])
                    em.op("dve", lambda e, bu=bu, oc=oc: e.tensor_mul(
                        hm[:, oc * 512:(oc + 1) * 512], sgb[:], pb[bu][:, 0:512]),
                        reads=["sgb", PB[bu]], writes=["hm"])
                for on in range(4):
                    for tc in range(4):
                        cg = blk * 4 + tc
                        em.op("sp", lambda e, cg=cg, on=on, tc=tc: e.dma_start(
                            out=a0r5[tc][:], in_=self.x_own[cg * 128:(cg + 1) * 128,
                                                            on * 512:(on + 1) * 512]),
                            writes=[f"a0r5{tc}"], dma=True)
                        em.op("sp", lambda e, cg=cg, on=on, tc=tc: e.dma_start(
                            out=r1s[tc][:], in_=self.r1_spill[cg * 128:(cg + 1) * 128,
                                                              on * 512:(on + 1) * 512]),
                            writes=[f"r1s{tc}"], dma=True)
                    for hp in range(32):
                        wi = hp % 4
                        dk = f"wdb{wi}"
                        em.op("pool", lambda e, wi=wi, hp=hp, on=on: e.dma_start(
                            out=wdb[wi][:], in_=self.w_down_t[on, hp]),
                            writes=[dk], dma=True)
                        for half in range(2):
                            hk = 2 * hp + half
                            for tc in range(4):
                                em.op("pe", lambda e, wi=wi, hk=hk, tc=tc, half=half:
                                      e.matmul(
                                          pb[tc][:, 0:512],
                                          hm[:, hk * 512 + tc * 128:
                                             hk * 512 + (tc + 1) * 128],
                                          wdb[wi][:, half * 512:(half + 1) * 512],
                                          start=(hk == 0), stop=(hk == 63)),
                                      reads=[dk, "hm"], writes=[PB[tc]],
                                      inc=(1 if tc == 3 else 0))
                    for tc in range(4):
                        cg = blk * 4 + tc
                        bi2 = tc % 2
                        em.op("dve", lambda e, tc=tc: e.tensor_add(
                            t5[:], a0r5[tc][:], r1s[tc][:]),
                            reads=[f"a0r5{tc}", f"r1s{tc}"], writes=["t5"])
                        em.op("dve", lambda e, tc=tc, bi2=bi2: e.tensor_add(
                            a1s[bi2][:], t5[:], pb[tc][:, 0:512]),
                            reads=["t5", PB[tc]], writes=[f"a1s{bi2}"])
                        em.op("sp", lambda e, cg=cg, on=on, bi2=bi2: e.dma_start(
                            out=self.a1_spill[cg * 128:(cg + 1) * 128,
                                              on * 512:(on + 1) * 512],
                            in_=a1s[bi2][:]),
                            reads=[f"a1s{bi2}"], writes=["a1_spill"], dma=True)
            if self.debug:
                em.op("sp", lambda e: e.dma_start(out=self.dbg["a1"], in_=self.a1_spill),
                      reads=["a1_spill"], dma=True)
            em.barrier()

    def phase6(self):
        em, nc = self.em, self.nc
        pb, pt, PB, PT = self.pb, self.pt, self.PB, self.PT
        with contextlib.ExitStack() as ph:
            def psb(name, shape, dt=F32):
                return ph.enter_context(nc.sbuf_tensor(name, list(shape), dt))

            hall = psb("hall", [128, 88 * 512], BF16)
            x2fm = psb("x2fm", [128, 16 * 512], BF16)
            x2ffm = psb("x2ffm", [128, 16 * 128])
            a1r = [psb(f"a1r{i}", [128, DM]) for i in range(1)]
            x2b = psb("x2b", [128, DM], BF16)
            x2f6 = psb("x2f6", [128, DM])
            sqs6 = psb("sqs6", [128, 1024])
            scc6 = psb("scc6", [128, 10])
            swr = psb("swr", [128, 16 * 8])
            rcol = psb("rcol", [128, 24])
            rsc = psb("rsc", [128, 32])
            cfmr = psb("cfmr", [8, 512])
            wgb6 = [psb(f"wgb6{i}", [128, 16 * 128], BF16) for i in range(3)]
            wub6 = [psb(f"wub6{i}", [128, 16 * 128], BF16) for i in range(3)]
            wdb6 = [psb(f"wdb6{i}", [128, 1024], BF16) for i in range(4)]
            sgb6 = psb("sgb6", [128, 512])
            hbuf = psb("hbuf", [128, 512])
            cb6 = psb("cb6", [128, 512])
            a1s6 = [psb(f"a1s6{i}", [128, 512]) for i in range(4)]
            outs6 = [psb(f"outs6{i}", [128, 512]) for i in range(1)]

            em.op("sp", lambda e: e.dma_start(out=swr[:], in_=self.wr),
                  writes=["swr"], dma=True)

            for blk in range(2):
                for tc in range(4):
                    cg = blk * 4 + tc
                    ri = 0
                    em.op("sp", lambda e, cg=cg, ri=ri: e.dma_start(
                        out=a1r[ri][:], in_=self.a1_spill[cg * 128:(cg + 1) * 128, :]),
                        writes=[f"a1r{ri}"], dma=True)
                    sm = scc6[:, 0:1]; sq = scc6[:, 1:2]; mc = scc6[:, 2:3]
                    vc = scc6[:, 3:4]; tmp = scc6[:, 4:5]; r1c = scc6[:, 5:6]
                    em.op("act", lambda e, ri=ri, sm=sm: e.activation(
                        sqs6[:], a1r[ri][:, 0:1024], AF.Copy, accum_out=sm),
                        reads=[f"a1r{ri}"], writes=["sqs6", "scc6a"])
                    em.op("act", lambda e, ri=ri: e.activation(
                        sqs6[:], a1r[ri][:, 1024:2048], AF.Copy, accum_out=scc6[:, 6:7]),
                        reads=[f"a1r{ri}"], writes=["sqs6", "scc6b"])
                    em.op("act", lambda e, ri=ri, sq=sq: e.activation(
                        sqs6[:], a1r[ri][:, 0:1024], AF.Square, accum_out=sq),
                        reads=[f"a1r{ri}"], writes=["sqs6", "scc6c"])
                    em.op("act", lambda e, ri=ri: e.activation(
                        sqs6[:], a1r[ri][:, 1024:2048], AF.Square, accum_out=scc6[:, 7:8]),
                        reads=[f"a1r{ri}"], writes=["sqs6", "scc6d"])
                    em.op("dve", lambda e, sm=sm: e.tensor_add(scc6[:, 8:9], sm, scc6[:, 6:7]),
                          reads=["scc6a", "scc6b"], writes=["scc6e"])
                    em.op("dve", lambda e, sq=sq: e.tensor_add(scc6[:, 9:10], sq, scc6[:, 7:8]),
                          reads=["scc6c", "scc6d"], writes=["scc6f"])
                    em.op("dve", lambda e, mc=mc: e.tensor_scalar(
                        mc, scc6[:, 8:9], 1.0 / DM, None, ALU.mult),
                        reads=["scc6e"], writes=["scc6g"])
                    em.op("dve", lambda e, mc=mc, tmp=tmp: e.tensor_mul(tmp, mc, mc),
                          reads=["scc6g"], writes=["scc6h"])
                    em.op("dve", lambda e, vc=vc, tmp=tmp: e.scalar_tensor_tensor(
                        vc, scc6[:, 9:10], 1.0 / DM, tmp, ALU.mult, ALU.subtract),
                        reads=["scc6f", "scc6h"], writes=["scc6i"])
                    em.op("act", lambda e, vc=vc, tmp=tmp: e.activation(
                        scc6[:, 4:5], vc, AF.Sqrt, bias=self.cfv("epscol")),
                        reads=["scc6i", "scf"], writes=["scc6j"])
                    em.op("dve", lambda e, r1c=r1c: e.reciprocal(r1c, scc6[:, 4:5]),
                          reads=["scc6j"], writes=["scc6k"])
                    em.op("dve", lambda e, ri=ri, mc=mc, r1c=r1c: e.tensor_scalar(
                        x2f6[:], a1r[ri][:], mc, r1c, ALU.subtract, ALU.mult),
                        reads=[f"a1r{ri}", "scc6g", "scc6k"], writes=["x2f6"])
                    em.op("dve", lambda e: e.tensor_copy(x2b[:], x2f6[:]),
                          reads=["x2f6"], writes=["x2b"])
                    for fc in range(16):
                        bk = PT[fc % 2]
                        em.op("pe", lambda e, fc=fc: e.transpose(
                            pt[fc % 2][:, 0:128], x2b[:, fc * 128:(fc + 1) * 128],
                            self.cbv("identb")),
                            reads=["x2b", "scb"], writes=[bk])
                        em.op("dve", lambda e, fc=fc, tc=tc: e.tensor_copy(
                            x2fm[:, fc * 512 + tc * 128: fc * 512 + (tc + 1) * 128],
                            pt[fc % 2][:, 0:128]),
                            reads=[bk], writes=["x2fm"])
                    for fc in range(16):
                        bk = PB[fc % 2]
                        em.op("pe", lambda e, fc=fc: e.transpose(
                            pb[fc % 2][:, 0:128], x2f6[:, fc * 128:(fc + 1) * 128],
                            self.cfv("identf")),
                            reads=["x2f6", "scf"], writes=[bk])
                        em.op("dve", lambda e, fc=fc: e.tensor_copy(
                            x2ffm[:, fc * 128:(fc + 1) * 128], pb[fc % 2][:, 0:128]),
                            reads=[bk], writes=["x2ffm"])
                    for k in range(16):
                        em.op("pe", lambda e, k=k: e.matmul(
                            pb[5][:, 0:8], x2ffm[:, k * 128:(k + 1) * 128],
                            swr[:, k * 8:(k + 1) * 8], start=(k == 0), stop=(k == 15)),
                            reads=["x2ffm", "swr"], writes=[PB[5]],
                            inc=(1 if k == 15 else 0))
                    m1 = rcol[:, 0:1]; nm1 = rcol[:, 1:2]; ge1 = rcol[:, 2:10]
                    msk = rcol[:, 10:18]; m2 = rcol[:, 18:19]
                    den = rcol[:, 19:20]; rden = rcol[:, 20:21]
                    em.op("dve", lambda e, nm1=nm1: e.reduce_max(
                        nm1, pb[5][:, 0:8], axis=AX.X, negate=True),
                        reads=[PB[5]], writes=["rcA"])
                    em.op("dve", lambda e, m1=m1, nm1=nm1: e.tensor_scalar(
                        m1, nm1, -1.0, None, ALU.mult), reads=["rcA"], writes=["rcB"])
                    em.op("dve", lambda e, m1=m1, ge1=ge1: e.tensor_scalar(
                        ge1, pb[5][:, 0:8], m1, None, ALU.is_ge),
                        reads=[PB[5], "rcB"], writes=["rcC"])
                    em.op("dve", lambda e, ge1=ge1, msk=msk: e.scalar_tensor_tensor(
                        msk, ge1, -1e30, pb[5][:, 0:8], ALU.mult, ALU.add),
                        reads=["rcC", PB[5]], writes=["rcD"])
                    em.op("dve", lambda e, msk=msk, m2=m2: e.reduce_max(
                        m2, msk, axis=AX.X), reads=["rcD"], writes=["rcE"])
                    em.op("act", lambda e, nm1=nm1: e.activation(
                        rsc[:, 0:8], pb[5][:, 0:8], AF.Exp, bias=nm1),
                        reads=[PB[5], "rcA"], writes=["rcF"])
                    em.op("dve", lambda e, m2=m2: e.tensor_scalar(
                        rsc[:, 8:16], pb[5][:, 0:8], m2, None, ALU.is_ge),
                        reads=[PB[5], "rcE"], writes=["rcG"])
                    em.op("dve", lambda e: e.tensor_mul(
                        rsc[:, 16:24], rsc[:, 0:8], rsc[:, 8:16]),
                        reads=["rcF", "rcG"], writes=["rcH"])
                    em.op("dve", lambda e, den=den: e.reduce_sum(
                        den, rsc[:, 16:24], axis=AX.X), reads=["rcH"], writes=["rcI"])
                    em.op("dve", lambda e, den=den, rden=rden: e.reciprocal(rden, den),
                          reads=["rcI"], writes=["rcJ"])
                    em.op("dve", lambda e, rden=rden: e.tensor_scalar(
                        rsc[:, 24:32], rsc[:, 16:24], rden, None, ALU.mult),
                        reads=["rcH", "rcJ"], writes=["rcK"])
                    em.op("pe", lambda e: e.transpose(
                        pb[4][0:8, 0:128], rsc[:, 24:32], self.cfv("identf")),
                        reads=["rcK", "scf"], writes=[PB[4]])
                    em.op("dve", lambda e, tc=tc: e.tensor_copy(
                        cfmr[0:8, tc * 128:(tc + 1) * 128], pb[4][0:8, 0:128]),
                        reads=[PB[4]], writes=["cfmr"])

                for e_ in range(NEXP):
                    em.op("pe", lambda e, e_=e_: e.matmul(
                        pb[4][:, 0:512], self.sel8v(e_), cfmr[0:8, :],
                        start=True, stop=True),
                        reads=["scf", "cfmr"], writes=[PB[4]])
                    em.op("act", lambda e: e.activation(
                        cb6[:], pb[4][:, 0:512], AF.Copy),
                        reads=[PB[4]], writes=["cb6"])
                    for oc in range(11):
                        wi = oc % 3
                        gk, uk = f"wgb6{wi}", f"wub6{wi}"
                        em.op("pool", lambda e, wi=wi, oc=oc, e_=e_: e.dma_start(
                            out=wgb6[wi][:], in_=self.wg_t[e_, oc]),
                            writes=[gk], dma=True)
                        em.op("pool", lambda e, wi=wi, oc=oc, e_=e_: e.dma_start(
                            out=wub6[wi][:], in_=self.wu_t[e_, oc]),
                            writes=[uk], dma=True)
                        bg, bu = 2 * wi, 2 * wi + 1
                        for k in range(16):
                            em.op("pe", lambda e, wi=wi, k=k, bg=bg: e.matmul(
                                pb[bg][:, 0:512], wgb6[wi][:, k * 128:(k + 1) * 128],
                                x2fm[:, k * 512:(k + 1) * 512],
                                start=(k == 0), stop=(k == 15)),
                                reads=[gk, "x2fm"], writes=[PB[bg]],
                                inc=(1 if k == 15 else 0))
                            em.op("pe", lambda e, wi=wi, k=k, bu=bu: e.matmul(
                                pb[bu][:, 0:512], wub6[wi][:, k * 128:(k + 1) * 128],
                                x2fm[:, k * 512:(k + 1) * 512],
                                start=(k == 0), stop=(k == 15)),
                                reads=[uk, "x2fm"], writes=[PB[bu]],
                                inc=(1 if k == 15 else 0))
                        em.op("act", lambda e, bg=bg: e.activation(
                            sgb6[:], pb[bg][:, 0:512], AF.Silu),
                            reads=[PB[bg]], writes=["sgb6"])
                        em.op("dve", lambda e, bu=bu: e.tensor_mul(
                            hbuf[:], sgb6[:], pb[bu][:, 0:512]),
                            reads=["sgb6", PB[bu]], writes=["hbuf"])
                        em.op("dve", lambda e, e_=e_, oc=oc: e.tensor_mul(
                            hall[:, (e_ * 11 + oc) * 512:(e_ * 11 + oc + 1) * 512],
                            hbuf[:], cb6[:]),
                            reads=["hbuf", "cb6"], writes=["hall"])
                for on in range(4):
                    for tc in range(4):
                        cg = blk * 4 + tc
                        em.op("sp", lambda e, cg=cg, on=on, tc=tc: e.dma_start(
                            out=a1s6[tc][:],
                            in_=self.a1_spill[cg * 128:(cg + 1) * 128,
                                              on * 512:(on + 1) * 512]),
                            writes=[f"a1s6{tc}"], dma=True)
                    for ep in range(44):
                        wi = ep % 4
                        dk = f"wdb6{wi}"
                        em.op("pool", lambda e, wi=wi, ep=ep, on=on: e.dma_start(
                            out=wdb6[wi][:], in_=self.wd_t[on, ep]),
                            writes=[dk], dma=True)
                        for half in range(2):
                            ek = 2 * ep + half
                            for tc in range(4):
                                em.op("pe", lambda e, wi=wi, ek=ek, tc=tc, half=half:
                                      e.matmul(
                                          pb[tc][:, 0:512],
                                          hall[:, ek * 512 + tc * 128:
                                               ek * 512 + (tc + 1) * 128],
                                          wdb6[wi][:, half * 512:(half + 1) * 512],
                                          start=(ek == 0), stop=(ek == 87)),
                                      reads=[dk, "hall"], writes=[PB[tc]],
                                      inc=(1 if tc == 3 else 0))
                    for tc in range(4):
                        cg = blk * 4 + tc
                        bi2 = 0
                        em.op("dve", lambda e, tc=tc, bi2=bi2: e.tensor_add(
                            outs6[bi2][:], a1s6[tc][:], pb[tc][:, 0:512]),
                            reads=[f"a1s6{tc}", PB[tc]], writes=[f"outs6{bi2}"])
                        em.op("sp", lambda e, cg=cg, on=on, bi2=bi2: e.dma_start(
                            out=self.out[cg * 128:(cg + 1) * 128,
                                         on * 512:(on + 1) * 512],
                            in_=outs6[bi2][:]),
                            reads=[f"outs6{bi2}"], writes=["out"], dma=True)
            em.barrier()

    def finish(self):
        em = self.em
        if self.upto < 6:
            with contextlib.ExitStack() as ph:
                ob = ph.enter_context(self.nc.sbuf_tensor("ob", [128, DM], F32))
                em.op("dve", lambda e: e.memset(ob[:], 0.0), writes=["ob"])
                for c in range(NCH):
                    em.op("sp", lambda e, c=c: e.dma_start(
                        out=self.out[c * 128:(c + 1) * 128, :], in_=ob[:]),
                        reads=["ob"], writes=["out"], dma=True)
        em.barrier()

        em2 = self.em
        with self.nc.Block() as block:
            @block.tensor
            def _(eng):
                for f in em2.ops["pe"]:
                    f(eng)

            @block.vector
            def _(eng):
                for f in em2.ops["dve"]:
                    f(eng)

            @block.scalar
            def _(eng):
                for f in em2.ops["act"]:
                    f(eng)

            @block.gpsimd
            def _(eng):
                for f in em2.ops["pool"]:
                    f(eng)

            @block.sync
            def _(eng):
                for f in em2.ops["sp"]:
                    f(eng)


def build_sel():
    sel = np.zeros((128, 64 * 128), np.float32)
    for j in range(64):
        sel[j, j * 128:(j + 1) * 128] = 1.0
    return sel


def make_in_maps(inputs, debug=False):
    CF, _, CB, _ = build_consts()
    SEL = build_sel()
    ac = np.ascontiguousarray
    w_in = np.asarray(inputs["L0_attn_in_proj_w"])
    w4 = w_in.reshape(16, 128, 10304)
    W_IN_P1 = ac(w4[:, :, 4096:10240].reshape(16, 128, 48, 128)
                 .transpose(2, 1, 0, 3).reshape(48, 128, 2048))
    W_IN_DT = ac(w4[:, :, 10240:10304].transpose(1, 0, 2).reshape(128, 1024))
    W_IN_Z = ac(w4[:, :, 0:4096].reshape(16, 128, 8, 512).transpose(2, 0, 1, 3))
    wo = np.asarray(inputs["L0_attn_out_proj_w"])
    W_OUT_T = ac(wo.reshape(32, 128, 4, 512).transpose(2, 0, 1, 3))
    wgm = np.asarray(inputs["L0_attn_gate_w"])
    W_GATE_T = ac(wgm.reshape(16, 128, 64, 128).transpose(2, 1, 0, 3).reshape(64, 128, 2048))
    wum = np.asarray(inputs["L0_attn_up_w"])
    W_UP_T = ac(wum.reshape(16, 128, 64, 128).transpose(2, 1, 0, 3).reshape(64, 128, 2048))
    wdm = np.asarray(inputs["L0_attn_down_w"])
    W_DOWN_T = ac(wdm.reshape(64, 128, 4, 512).transpose(2, 0, 1, 3)
                  .reshape(4, 32, 2, 128, 512).transpose(0, 1, 3, 2, 4)
                  .reshape(4, 32, 128, 1024))
    wge = np.asarray(inputs["L0_ffn_Wg"])
    WG_T = ac(wge.reshape(8, 16, 128, 11, 128).transpose(0, 3, 2, 1, 4).reshape(8, 11, 128, 2048))
    wue = np.asarray(inputs["L0_ffn_Wu"])
    WU_T = ac(wue.reshape(8, 16, 128, 11, 128).transpose(0, 3, 2, 1, 4).reshape(8, 11, 128, 2048))
    wde = np.asarray(inputs["L0_ffn_Wd"])
    WD_T = ac(wde.reshape(8, 11, 128, 4, 512).transpose(3, 0, 1, 2, 4)
              .reshape(4, 44, 2, 128, 512).transpose(0, 1, 3, 2, 4)
              .reshape(4, 44, 128, 1024))
    wrm = np.asarray(inputs["L0_ffn_Wr"])
    WR_T = ac(wrm.reshape(16, 128, 8).transpose(1, 0, 2).reshape(128, 128))
    x0 = np.asarray(inputs["L0_attn_hidden_states"])[0]
    maps = []
    for k in range(NCORES):
        x_own = np.ascontiguousarray(x0[k * T:(k + 1) * T])
        if k == 0:
            x_ctx = np.zeros((128, DM), np.float32)
        else:
            x_ctx = np.ascontiguousarray(x0[k * T - 128:k * T])
        fm = np.zeros((1, 16), np.float32)
        fm[0, :k] = 1.0
        fm[0, 8:] = 1.0 - fm[0, :8]
        m = {
            "x_own": x_own,
            "x_ctx": x_ctx,
            "w_in_p1": W_IN_P1,
            "w_in_dt": W_IN_DT,
            "w_in_z": W_IN_Z,
            "conv_w": np.ascontiguousarray(
                np.asarray(inputs["L0_attn_conv_w"]).reshape(48, 128, DCONV)
                .transpose(1, 0, 2).reshape(128, 48 * DCONV)),
            "conv_b": np.ascontiguousarray(
                np.asarray(inputs["L0_attn_conv_b"]).reshape(48, 128).T),
            "dt_bias": np.asarray(inputs["L0_attn_dt_bias"]).reshape(NH, 1),
            "a_log": np.asarray(inputs["L0_attn_A_log"]).reshape(NH, 1),
            "d_vec": np.asarray(inputs["L0_attn_D"]).reshape(1, NH),
            "norm_w": np.ascontiguousarray(
                np.asarray(inputs["L0_attn_norm_w"]).reshape(32, 128).T),
            "w_out_t": W_OUT_T,
            "ln1_w": np.asarray(inputs["L0_attn_ln1_w"]).reshape(1, DM),
            "ln2_w": np.asarray(inputs["L0_attn_ln2_w"]).reshape(1, DM),
            "w_gate_t": W_GATE_T,
            "w_up_t": W_UP_T,
            "w_down_t": W_DOWN_T,
            "wr": WR_T,
            "wg_t": WG_T,
            "wu_t": WU_T,
            "wd_t": WD_T,
            "constf": CF,
            "constb": CB,
            "constsel": SEL,
            "foldmask": fm,
        }
        maps.append(m)
    return maps


def kernel(**inputs):
    b = Builder(debug=False)
    nc = b.build()
    maps = make_in_maps(inputs)
    res = run_bass_kernel_spmd(nc, maps, core_ids=list(range(NCORES)))
    outp = np.concatenate([np.asarray(res.results[k]["out"]) for k in range(NCORES)], 0)
    return outp.reshape(1, 8192, DM)


# version 09 (final)
# speedup vs baseline: 1.1293x; 1.1293x over previous
import contextlib

import numpy as np
import ml_dtypes

import concourse.bass as bass
import concourse.mybir as mybir
from concourse.bass_utils import run_bass_kernel_spmd

F32 = mybir.dt.float32
BF16 = mybir.dt.bfloat16
AF = mybir.ActivationFunctionType
ALU = mybir.AluOpType
AX = mybir.AxisListType

NCORES = 8
T = 1024
NCH = 8
DM = 2048
DI = 4096
DXB = 1024
DST = 64
HD = 64
NH = 64
NG = 16
DCONV = 4
CONV_DIM = 6144
DFF = 8192
NEXP = 8
DEXP = 1408
EPS = 1e-5

STREAMS = ("pe", "dve", "act", "pool", "sp")


class Emitter:

    NDMA = {"sp": 8, "pool": 8, "act": 2}

    def __init__(self, nc, stack):
        self.nc = nc
        self.ops = {s: [] for s in STREAMS}
        self.chans = list(STREAMS) + ["cc"]
        for st, n in self.NDMA.items():
            self.chans += [f"{st}_d{i}" for i in range(n)]
        self.sems = {ch: stack.enter_context(nc.semaphore(f"sem_{ch}"))
                     for ch in self.chans}
        self.count = {ch: 0 for ch in self.chans}
        self.dma_rr = {st: 0 for st in self.NDMA}
        self.res = {}
        self.wm = {s: {ch: 0 for ch in self.chans} for s in STREAMS}
        self.snap = {ch: {0: {c2: 0 for c2 in self.chans}} for ch in self.chans}
        self.pending = {s: [] for s in STREAMS}
        self.oplog = []

    def _wait(self, stream, chan, val, is_dma):
        if val <= 0:
            return
        if self.wm[stream].get(chan, 0) >= val:
            return
        sem = self.sems[chan]
        self.ops[stream].append(lambda eng, sem=sem, v=val: eng.wait_ge(sem, v))
        self.wm[stream][chan] = val
        snap = self.snap[chan]
        best = None
        for v in snap:
            if v <= val and (best is None or v > best):
                best = v
        if best is not None:
            for c2, v2 in snap[best].items():
                if self.wm[stream].get(c2, 0) < v2:
                    self.wm[stream][c2] = v2

    def op(self, stream, fn, reads=(), writes=(), inc=1, dma=False, cc=False):
        if cc:
            chan = "cc"
            inc = 1
        elif dma:
            i = self.dma_rr[stream] % self.NDMA[stream]
            self.dma_rr[stream] += 1
            chan = f"{stream}_d{i}"
            inc = 16
        else:
            chan = stream
        for key in reads:
            r = self.res.get(key)
            if r is None:
                continue
            ch, c, isd = r["w"]
            if ch is not None:
                assert c is not None, f"read {key}: producer on {ch} lacks inc"
                self._wait(stream, ch, c, isd)
        for key in writes:
            for s2 in STREAMS:
                if s2 != stream:
                    for pk, pw in self.pending[s2]:
                        assert pk != key, (
                            f"write of {key} races pending (inc=0) access on {s2}")
            r = self.res.get(key)
            if r is None:
                continue
            ch, c, isd = r["w"]
            if ch is not None:
                assert c is not None, f"write {key}: producer lacks inc"
                self._wait(stream, ch, c, isd)
            for ch2, c2, isd2 in r["r"]:
                assert c2 is not None, f"write {key}: reader on {ch2} lacks inc"
                self._wait(stream, ch2, c2, isd2)

        if inc:
            if dma or cc:
                self._wait(stream, chan, self.count[chan], True)
            newc = self.count[chan] + inc
            sem = self.sems[chan]

            def wrapped(eng, fn=fn, sem=sem, inc=inc):
                ins = fn(eng)
                assert ins is not None, "op with inc must return an instruction"
                ins.then_inc(sem, inc)

            import traceback
            fr = traceback.extract_stack(limit=3)[0]
            self.oplog.append((chan, newc, f"{fr.filename.split('/')[-1]}:{fr.lineno}"))
            self.ops[stream].append(wrapped)
            self.count[chan] = newc
            for key, is_w in self.pending[stream]:
                self._record(key, is_w, chan, newc, dma)
            self.pending[stream] = []
            for key in reads:
                self._record(key, False, chan, newc, dma)
            for key in writes:
                self._record(key, True, chan, newc, dma)
            self.snap[chan][newc] = dict(self.wm[stream])
        else:
            assert not dma
            self.ops[stream].append(lambda eng, fn=fn: fn(eng))
            for key in reads:
                self.pending[stream].append((key, False))
            for key in writes:
                self.pending[stream].append((key, True))

    def _record(self, key, is_write, chan, count, is_dma):
        r = self.res.setdefault(key, {"w": (None, None, False), "r": []})
        if is_write:
            r["w"] = (chan, count, is_dma)
            r["r"] = []
        else:
            r["r"].append((chan, count, is_dma))

    def barrier(self, streams=STREAMS, skip_cc=False):
        vals = {}
        for s in streams:
            sem = self.sems[s]
            self.count[s] += 1
            c = self.count[s]
            self.ops[s].append(lambda eng, sem=sem: eng.drain(fusable=False).then_inc(sem, 1))
            self.snap[s][c] = dict(self.wm[s])
            vals[s] = c
        dma_chans = [ch for ch in self.chans
                     if "_d" in ch or (ch == "cc" and not skip_cc)]
        dma_vals = {ch: self.count[ch] for ch in dma_chans}
        for s in streams:
            for s2 in streams:
                if s2 != s:
                    self._wait(s, s2, vals[s2], False)
            for ch, v in dma_vals.items():
                self._wait(s, ch, v, True)


def build_consts():
    cf = {}
    cb = {}
    colf = []
    colb = []

    def addf(name, arr):
        arr = np.asarray(arr, np.float32)
        cf[name] = (sum(a.shape[1] for a in colf), arr.shape[1])
        colf.append(arr)

    def addb(name, arr):
        arr = np.asarray(arr, np.float32).astype(ml_dtypes.bfloat16)
        cb[name] = (sum(a.shape[1] for a in colb), arr.shape[1])
        colb.append(arr)

    ident = np.eye(128, dtype=np.float32)
    addf("identf", ident)
    addb("identb", ident)
    s = np.arange(128)
    addf("tril", (s[:, None] <= s[None, :]).astype(np.float32))
    onesr = np.zeros((128, 128), np.float32)
    onesr[0, :] = 1.0
    addf("onesrow", onesr)
    addf("onescol", np.ones((128, 1), np.float32))
    addf("epscol", np.full((128, 1), EPS, np.float32))
    addf("onecol", np.ones((128, 1), np.float32))

    sel8 = np.zeros((128, 8 * 128), np.float32)
    for e in range(8):
        sel8[e, e * 128:(e + 1) * 128] = 1.0
    addf("sel8", sel8)
    addf("zeros32", np.zeros((128, 32), np.float32))
    trilneg = np.where(np.arange(128)[:, None] > np.arange(128)[None, :],
                       np.float32(-1e30), np.float32(0.0))
    addf("trilneg", trilneg.astype(np.float32))
    return np.concatenate(colf, 1), cf, np.concatenate(colb, 1), cb


class Builder:
    def __init__(self, debug=False, n_cores=NCORES, upto=99, p2_chunks=NCH, p2_tail=True):
        self.debug = debug
        self.n_cores = n_cores
        self.upto = upto
        self.p2_chunks = p2_chunks
        self.p2_tail = p2_tail
        self.p2_parts = 7
        self.p3_chunks = NCH
        self.p3_parts = 0xfff
        self.CF, self.cfm, self.CB, self.cbm = build_consts()
        self.nc = bass.Bass("TRN2", target_bir_lowering=False)

    def cfv(self, name, rows=128):
        off, w = self.cfm[name]
        return self.scf[0:rows, off:off + w]

    def cbv(self, name, rows=128):
        off, w = self.cbm[name]
        return self.scb[0:rows, off:off + w]

    def selv(self, h):
        return self.ssel[0:64, h * 128:(h + 1) * 128]

    def sel8v(self, e):
        off, _ = self.cfm["sel8"]
        return self.scf[0:8, off + e * 128: off + (e + 1) * 128]

    def stat(self, c, i):
        return self.stats[:, c * 8 + i: c * 8 + i + 1]

    def build(self):
        nc = self.nc
        dbg = self.debug

        def din(name, shape, dt=F32):
            return nc.dram_tensor(name, list(shape), dt, kind="ExternalInput").ap()

        self.x_own = din("x_own", [T, DM])
        self.x_ctx = din("x_ctx", [128, DM])
        self.w_in_p1 = din("w_in_p1", [48, 128, 2048])
        self.w_in_dt = din("w_in_dt", [128, 1024])
        self.w_in_z = din("w_in_z", [8, 16, 128, 512])
        self.conv_w = din("conv_w", [128, 48 * DCONV])
        self.conv_b = din("conv_b", [128, 48])
        self.dt_bias = din("dt_bias", [NH, 1])
        self.a_log = din("a_log", [NH, 1])
        self.d_vec = din("d_vec", [1, NH])
        self.norm_w = din("norm_w", [128, 32])
        self.w_out_t = din("w_out_t", [4, 32, 128, 512])
        self.ln1_w = din("ln1_w", [1, DM])
        self.ln2_w = din("ln2_w", [1, DM])
        self.w_gate_t = din("w_gate_t", [64, 128, 2048])
        self.w_up_t = din("w_up_t", [64, 128, 2048])
        self.w_down_t = din("w_down_t", [4, 32, 128, 1024])
        self.wr = din("wr", [128, 16 * NEXP])
        self.wg_t = din("wg_t", [NEXP, 11, 128, 2048])
        self.wu_t = din("wu_t", [NEXP, 11, 128, 2048])
        self.wd_t = din("wd_t", [4, 44, 128, 1024])
        self.constf = din("constf", [128, self.CF.shape[1]])
        self.constsel = din("constsel", [128, 64 * 128])
        self.constb = din("constb", [128, self.CB.shape[1]], BF16)
        self.foldmask = din("foldmask", [1, 16])
        self.out = nc.dram_tensor("out", [T, DM], F32, kind="ExternalOutput").ap()

        self.conv_spill = nc.dram_tensor("conv_spill", [CONV_DIM, T], F32, kind="Internal").ap()
        self.z_spill = nc.dram_tensor("z_spill", [T, DI], F32, kind="Internal").ap()
        self.yfm_spill = nc.dram_tensor("yfm_spill", [DI, T], BF16, kind="Internal").ap()
        self.r1_spill = nc.dram_tensor("r1_spill", [T, DM], F32, kind="Internal").ap()
        self.a1_spill = nc.dram_tensor("a1_spill", [T, DM], F32, kind="Internal").ap()
        self.cc_in = nc.dram_tensor("cc_in", [65, 4096], F32, kind="Internal")
        self.cc_out = nc.dram_tensor("cc_out", [self.n_cores * 65, 4096], F32, kind="Internal")

        self.dbg = {}
        if dbg:
            self.dbg["conv"] = nc.dram_tensor("dbg_conv", [CONV_DIM, T], F32, kind="ExternalOutput").ap()
            self.dbg["dt"] = nc.dram_tensor("dbg_dt", [NH, T], F32, kind="ExternalOutput").ap()
            self.dbg["z"] = nc.dram_tensor("dbg_z", [T, DI], F32, kind="ExternalOutput").ap()
            self.dbg["y"] = nc.dram_tensor("dbg_y", [DI, T], BF16, kind="ExternalOutput").ap()
            self.dbg["r1"] = nc.dram_tensor("dbg_r1", [T, DM], F32, kind="ExternalOutput").ap()
            self.dbg["a1"] = nc.dram_tensor("dbg_a1", [T, DM], F32, kind="ExternalOutput").ap()
            self.dbg["prevs"] = nc.dram_tensor("dbg_prevs", [64, NH * HD], F32, kind="ExternalOutput").ap()
            self.dbg["xfmctx"] = nc.dram_tensor("dbg_xfmctx", [128, 16 * 128], F32, kind="ExternalOutput").ap()
            self.dbg["scc"] = nc.dram_tensor("dbg_scc", [128, 12], F32, kind="ExternalOutput").ap()
            self.dbg["dacT"] = nc.dram_tensor("dbg_dacT", [128, 512], F32, kind="ExternalOutput").ap()
            self.dbg["edacT"] = nc.dram_tensor("dbg_edacT", [128, 512], F32, kind="ExternalOutput").ap()
            self.dbg["dachm"] = nc.dram_tensor("dbg_dachm", [64, 1024], F32, kind="ExternalOutput").ap()
            self.dbg["t1b"] = nc.dram_tensor("dbg_t1b", [128, DM], F32, kind="ExternalOutput").ap()
            self.dbg["dts"] = nc.dram_tensor("dbg_dts", [64, 3 * T], F32, kind="ExternalOutput").ap()

        with contextlib.ExitStack() as stack:
            self.stack = stack
            em = Emitter(nc, stack)
            self.em = em

            def sbuf(name, shape, dt=F32):
                return stack.enter_context(nc.sbuf_tensor(name, list(shape), dt))

            def psum(name, shape, dt=F32):
                return stack.enter_context(nc.psum_tensor(name, list(shape), dt))

            self.sbuf = sbuf
            self.psum = psum

            self.scf = sbuf("scf", [128, self.CF.shape[1]])
            self.scb = sbuf("scb", [128, self.CB.shape[1]], BF16)
            self.stats = sbuf("stats", [128, 8 * NCH])
            self.normw_sb = sbuf("normw_sb", [128, 32])
            self.lnw_b = sbuf("lnw_b", [128, DM])
            self.ssdstack = contextlib.ExitStack()
            def ssdbuf(name, shape, dt=F32):
                return self.ssdstack.enter_context(nc.sbuf_tensor(name, list(shape), dt))
            self.dt_hm = ssdbuf("dt_hm", [64, T])
            self.acol = ssdbuf("acol", [64, 4])
            self.dbcast = ssdbuf("dbcast", [128, NH])
            self.dAcT = ssdbuf("dAcT", [128, NCH * NH])
            self.ndAcT = ssdbuf("ndAcT", [128, NCH * NH])
            self.edAcT = ssdbuf("edAcT", [128, NCH * NH])
            self.dtT = ssdbuf("dtT", [128, NCH * NH])
            self.decT = ssdbuf("decT", [128, NCH * NH])
            self.cdb = ssdbuf("cdb", [64, NCH * NH])
            self.dAc_hm = ssdbuf("dAc_hm", [64, NCH * 128])
            self.prevs = [ssdbuf("prevs0", [64, NH * HD]), ssdbuf("prevs1", [64, NH * HD])]
            self.sumlast = ssdbuf("sumlast", [1, 128])
            self.ssel = ssdbuf("ssel", [128, 64 * 128])

            self.pb = [psum(f"pb{i}", [128, 512]) for i in range(6)]
            self.pt = [psum(f"pt{i}", [128, 1024], BF16) for i in range(2)]
            self.PB = [f"pb{i}" for i in range(6)]
            self.PT = [f"pt{i}" for i in range(2)]

            self.xstack = contextlib.ExitStack()
            self.x_fm = self.xstack.enter_context(
                nc.sbuf_tensor("x_fm", [128, 16 * T], BF16))
            self.xfm_ctx = self.xstack.enter_context(
                nc.sbuf_tensor("xfm_ctx", [128, 16 * 128], BF16))

            em.op("sp", lambda e: e.dma_start(out=self.scf[:], in_=self.constf),
                  writes=["scf"], dma=True)
            em.op("sp", lambda e: e.dma_start(out=self.scb[:], in_=self.constb),
                  writes=["scb"], dma=True)
            em.op("sp", lambda e: e.dma_start(out=self.ssel[:], in_=self.constsel),
                  writes=["ssel"], dma=True)

            if not getattr(self, "skip_p0", False):
                self.phase0()
            if not getattr(self, "skip_p1", False):
                self.phase1()
            if self.upto >= 2:
                self.phase2()
            self.phase2fold()
            if self.upto >= 4:
                self.phase3()
            else:
                self.xstack.close()
            if self.upto >= 5:
                self.phase5()
            else:
                self.ssdstack.close()
            if self.upto >= 6:
                self.phase6()
            self.finish()
        return nc

    def phase0(self):
        em, nc = self.em, self.nc
        pb, pt, PB, PT = self.pb, self.pt, self.PB, self.PT
        with contextlib.ExitStack() as ph:
            def psb(name, shape, dt=F32):
                return ph.enter_context(nc.sbuf_tensor(name, list(shape), dt))

            a0b = psb("a0b", [128, 2 * DM])
            t1f = psb("t1f", [128, DM])
            t1b = psb("t1b", [128, DM], BF16)
            sqs = psb("sqs", [128, DM])
            scc = psb("scc", [128, 12])
            lnrow = psb("lnrow", [1, 2 * DM])
            ln1b = psb("ln1b", [128, DM])

            em.op("sp", lambda e: e.dma_start(out=lnrow[0:1, 0:DM], in_=self.ln1_w),
                  writes=["lnrow"], dma=True)
            em.op("sp", lambda e: e.dma_start(out=lnrow[0:1, DM:2 * DM], in_=self.ln2_w),
                  writes=["lnrow"], dma=True)
            for j in range(8):
                em.op("pe", lambda e, j=j: e.matmul(
                    pb[j % 2][:], self.cfv("onesrow")[0:1, :],
                    lnrow[0:1, j * 512:(j + 1) * 512], start=True, stop=True),
                    reads=["scf", "lnrow"], writes=[PB[j % 2]])
                if j < 4:
                    em.op("dve", lambda e, j=j: e.tensor_copy(
                        ln1b[:, j * 512:(j + 1) * 512], pb[j % 2][:]),
                        reads=[PB[j % 2]], writes=["ln1b"])
                else:
                    em.op("dve", lambda e, j=j: e.tensor_copy(
                        self.lnw_b[:, (j - 4) * 512:(j - 3) * 512], pb[j % 2][:]),
                        reads=[PB[j % 2]], writes=["lnw_b"])

            for c in range(NCH + 1):
                ctx = (c == NCH)
                bi = c % 2
                buf = a0b[:, bi * DM:(bi + 1) * DM]
                bufk = f"a0b{bi}"
                src = self.x_ctx if ctx else self.x_own[c * 128:(c + 1) * 128, :]
                em.op("sp", lambda e, buf=buf, src=src: e.dma_start(out=buf, in_=src),
                      writes=[bufk], dma=True)
                sm = scc[:, 0:1]; sq = scc[:, 1:2]
                mcol = scc[:, 2:3]; vcol = scc[:, 3:4]; ve = scc[:, 4:5]
                r1c = scc[:, 5:6]; rcomb = scc[:, 6:7]; tmp = scc[:, 7:8]
                ve2 = scc[:, 8:9]; ve3 = scc[:, 9:10]; ve4 = scc[:, 10:11]; ve5 = scc[:, 11:12]
                em.op("act", lambda e, buf=buf, sm=sm: e.activation(
                    sqs[:], buf, AF.Copy, accum_out=sm),
                    reads=[bufk], writes=["sqs", "scc"])
                em.op("act", lambda e, buf=buf, sq=sq: e.activation(
                    sqs[:], buf, AF.Square, accum_out=sq),
                    reads=[bufk], writes=["sqs", "scc"])
                em.op("dve", lambda e, sm=sm, mcol=mcol: e.tensor_scalar(
                    mcol, sm, 1.0 / DM, None, ALU.mult), reads=["scc"], writes=["scc"])
                em.op("dve", lambda e, mcol=mcol, tmp=tmp: e.tensor_mul(tmp, mcol, mcol),
                      reads=["scc"], writes=["scc"])
                em.op("dve", lambda e, sq=sq, tmp=tmp, vcol=vcol: e.scalar_tensor_tensor(
                    vcol, sq, 1.0 / DM, tmp, ALU.mult, ALU.subtract),
                    reads=["scc"], writes=["scc"])
                em.op("act", lambda e, vcol=vcol, tmp=tmp: e.activation(
                    tmp, vcol, AF.Sqrt, bias=self.cfv("epscol")),
                    reads=["scc", "scf"], writes=["scc"])
                em.op("dve", lambda e, tmp=tmp, r1c=r1c: e.reciprocal(r1c, tmp),
                      reads=["scc"], writes=["scc"])
                em.op("dve", lambda e, vcol=vcol, ve=ve: e.tensor_scalar(
                    ve, vcol, EPS, None, ALU.add), reads=["scc"], writes=["scc"])
                em.op("dve", lambda e, ve=ve, ve2=ve2: e.reciprocal(ve2, ve), reads=["scc"], writes=["scc"])
                em.op("dve", lambda e, vcol=vcol, ve2=ve2, ve3=ve3: e.tensor_mul(ve3, vcol, ve2),
                      reads=["scc"], writes=["scc"])
                em.op("act", lambda e, ve3=ve3, ve4=ve4: e.activation(
                    ve4, ve3, AF.Sqrt, bias=self.cfv("epscol")),
                    reads=["scc", "scf"], writes=["scc"])
                em.op("dve", lambda e, ve4=ve4, ve5=ve5: e.reciprocal(ve5, ve4), reads=["scc"], writes=["scc"])
                em.op("dve", lambda e, r1c=r1c, ve5=ve5, rcomb=rcomb: e.tensor_mul(rcomb, r1c, ve5),
                      reads=["scc"], writes=["scc"])
                if not ctx:
                    em.op("dve", lambda e, mcol=mcol, c=c: e.tensor_copy(self.stat(c, 0), mcol),
                          reads=["scc"], writes=["stats"])
                    em.op("dve", lambda e, r1c=r1c, c=c: e.tensor_copy(self.stat(c, 1), r1c),
                          reads=["scc"], writes=["stats"])
                em.op("dve", lambda e, buf=buf, mcol=mcol, rcomb=rcomb: e.tensor_scalar(
                    t1f[:], buf, mcol, rcomb, ALU.subtract, ALU.mult),
                    reads=[bufk, "scc"], writes=["t1f"])
                em.op("dve", lambda e: e.tensor_mul(t1b[:], t1f[:], ln1b[:]),
                      reads=["t1f", "ln1b"], writes=["t1b"])
                if ctx and self.debug:
                    em.op("pool", lambda e: e.dma_start(out=self.dbg["scc"], in_=scc[:]),
                          reads=["scc"], dma=True)
                    em.op("pool", lambda e: e.dma_start(out=self.dbg["t1b"], in_=t1b[:]),
                          reads=["t1b"], dma=True)
                for fc in range(16):
                    ptk = PT[fc % 2]
                    em.op("pe", lambda e, fc=fc: e.transpose(
                        pt[fc % 2][:, 0:128], t1b[:, fc * 128:(fc + 1) * 128],
                        self.cbv("identb")),
                        reads=["t1b", "scb"], writes=[ptk])
                    if ctx:
                        em.op("dve", lambda e, fc=fc: e.tensor_copy(
                            self.xfm_ctx[:, fc * 128:(fc + 1) * 128], pt[fc % 2][:, 0:128]),
                            reads=[ptk], writes=["xfm_ctx"])
                    else:
                        em.op("dve", lambda e, fc=fc, c=c: e.tensor_copy(
                            self.x_fm[:, fc * T + c * 128: fc * T + (c + 1) * 128],
                            pt[fc % 2][:, 0:128]),
                            reads=[ptk], writes=["x_fm"])
            em.barrier()

    def phase1(self):
        em, nc = self.em, self.nc
        pb, PB = self.pb, self.PB
        with contextlib.ExitStack() as ph:
            def psb(name, shape, dt=F32):
                return ph.enter_context(nc.sbuf_tensor(name, list(shape), dt))

            wb = [psb(f"wb{i}", [128, 16 * 128], BF16) for i in range(2)]
            convin = [psb(f"ci{i}", [128, 1152]) for i in range(2)]
            cvt = psb("cvt", [128, T])
            cvo = [psb(f"cvo{i}", [128, T]) for i in range(2)]
            cw = psb("cw", [128, 48 * DCONV])
            cb_ = psb("cb", [128, 48])
            drow = psb("drow", [1, NH])
            dtraw = psb("dtraw", [64, T])
            dts = psb("dts", [64, 3 * T])

            em.op("sp", lambda e: e.dma_start(out=cw[:], in_=self.conv_w),
                  writes=["cw"], dma=True)
            em.op("sp", lambda e: e.dma_start(out=cb_[:], in_=self.conv_b),
                  writes=["cb"], dma=True)
            em.op("sp", lambda e: e.dma_start(out=self.acol[:, 0:1], in_=self.a_log),
                  writes=["acol"], dma=True)
            em.op("sp", lambda e: e.dma_start(out=self.acol[:, 2:3], in_=self.dt_bias),
                  writes=["acol"], dma=True)
            em.op("sp", lambda e: e.dma_start(out=drow[:], in_=self.d_vec),
                  writes=["drow"], dma=True)
            em.op("act", lambda e: e.activation(self.acol[:, 1:2], self.acol[:, 0:1], AF.Exp),
                  reads=["acol"], writes=["acol"])
            em.op("dve", lambda e: e.tensor_scalar(
                self.acol[:, 1:2], self.acol[:, 1:2], -1.0, None, ALU.mult),
                reads=["acol"], writes=["acol"])
            em.op("pe", lambda e: e.matmul(pb[5][:, 0:NH], self.cfv("onesrow")[0:1, :],
                                           drow[0:1, :], start=True, stop=True),
                  reads=["scf", "drow"], writes=[PB[5]])
            em.op("dve", lambda e: e.tensor_copy(self.dbcast[:], pb[5][:, 0:NH]),
                  reads=[PB[5]], writes=["dbcast"])

            for oc in range(49):
                is_dt = (oc == 48)
                col0 = 4096 + oc * 128 if not is_dt else 10240
                ncol = 128 if not is_dt else 64
                bi = oc % 2
                wkey = f"wb{bi}"
                if is_dt:
                    em.op("pool", lambda e, bi=bi: e.dma_start(
                        out=wb[bi][:, 0:1024], in_=self.w_in_dt),
                        writes=[wkey], dma=True)
                else:
                    em.op("pool", lambda e, bi=bi, oc=oc: e.dma_start(
                        out=wb[bi][:], in_=self.w_in_p1[oc]),
                        writes=[wkey], dma=True)
                b0 = 0 if bi == 0 else 3
                bA, bB, bC = pb[b0], pb[b0 + 1], pb[b0 + 2]
                kA, kB, kC = PB[b0], PB[b0 + 1], PB[b0 + 2]
                for k in range(16):
                    st = (k == 0)
                    sp_ = (k == 15)
                    lhs = lambda k=k, bi=bi, ncol=ncol: wb[bi][:, k * ncol:(k + 1) * ncol]
                    em.op("pe", lambda e, k=k, lhs=lhs, bA=bA, ncol=ncol, st=st, sp_=sp_: e.matmul(
                        bA[0:ncol, 0:512], lhs(), self.x_fm[:, k * T:k * T + 512],
                        start=st, stop=sp_),
                        reads=[wkey, "x_fm"], writes=[kA], inc=0)
                    em.op("pe", lambda e, k=k, lhs=lhs, bB=bB, ncol=ncol, st=st, sp_=sp_: e.matmul(
                        bB[0:ncol, 0:512], lhs(), self.x_fm[:, k * T + 512:k * T + 1024],
                        start=st, stop=sp_),
                        reads=[wkey, "x_fm"], writes=[kB],
                        inc=(1 if (sp_ and is_dt) else 0))
                    if not is_dt:
                        em.op("pe", lambda e, k=k, lhs=lhs, bC=bC, st=st, sp_=sp_: e.matmul(
                            bC[0:128, 0:128], lhs(), self.xfm_ctx[:, k * 128:(k + 1) * 128],
                            start=st, stop=sp_),
                            reads=[wkey, "xfm_ctx"], writes=[kC],
                            inc=(1 if sp_ else 0))

                if is_dt:
                    em.op("dve", lambda e: e.tensor_scalar(
                        dtraw[:, 0:512], bA[0:64, 0:512], self.acol[0:64, 2:3], None, ALU.add),
                        reads=[kA, "acol"], writes=["dtraw"])
                    em.op("dve", lambda e: e.tensor_scalar(
                        dtraw[:, 512:1024], bB[0:64, 0:512], self.acol[0:64, 2:3], None, ALU.add),
                        reads=[kB, "acol"], writes=["dtraw"])
                    r0 = dts[:, 0:T]
                    r1_ = dts[:, T:2 * T]
                    em.op("act", lambda e, r0=r0: e.activation(r0, dtraw[:], AF.Abs),
                          reads=["dtraw"], writes=["dts0"])
                    em.op("act", lambda e, r0=r0, r1_=r1_: e.activation(r1_, r0, AF.Exp, scale=-1.0),
                          reads=["dts0"], writes=["dts1"])
                    em.op("act", lambda e, r0=r0, r1_=r1_: e.activation(
                        r0, r1_, AF.Ln, bias=self.cfv("onecol", 64)),
                        reads=["dts1", "scf"], writes=["dts0"])
                    r2 = dts[:, 2 * T:3 * T]
                    em.op("dve", lambda e, r2=r2: e.tensor_scalar(
                        r2, dtraw[:], 0.0, None, ALU.max), reads=["dtraw"], writes=["dts2"])
                    em.op("dve", lambda e, r0=r0, r2=r2: e.tensor_add(self.dt_hm[:], r2, r0),
                          reads=["dts0", "dts2"], writes=["dt_hm"])
                else:
                    ck = f"ci{bi}"
                    em.op("act", lambda e, bi=bi, bC=bC: e.activation(
                        convin[bi][:, 0:128], bC[0:128, 0:128], AF.Copy),
                        reads=[kC], writes=[ck])
                    em.op("act", lambda e, bi=bi, bA=bA: e.activation(
                        convin[bi][:, 128:640], bA[0:128, 0:512], AF.Copy),
                        reads=[kA], writes=[ck])
                    em.op("act", lambda e, bi=bi, bB=bB: e.activation(
                        convin[bi][:, 640:1152], bB[0:128, 0:512], AF.Copy),
                        reads=[kB], writes=[ck])
                    wc = lambda k, oc=oc: cw[:, oc * DCONV + k: oc * DCONV + k + 1]
                    bcol = cb_[:, oc:oc + 1]
                    em.op("dve", lambda e, bi=bi, wc=wc, bcol=bcol: e.tensor_scalar(
                        cvt[:], convin[bi][:, 125:1149], wc(0), bcol, ALU.mult, ALU.add),
                        reads=[ck, "cw", "cb"], writes=["cvt"])
                    for k in range(1, 4):
                        em.op("dve", lambda e, bi=bi, k=k, wc=wc: e.scalar_tensor_tensor(
                            cvt[:], convin[bi][:, 125 + k:1149 + k], wc(k), cvt[:],
                            ALU.mult, ALU.add),
                            reads=[ck, "cw", "cvt"], writes=["cvt"])
                    em.op("act", lambda e, bi=bi: e.activation(cvo[bi][:], cvt[:], AF.Silu),
                          reads=["cvt"], writes=[f"cvo{bi}"])
                    em.op("sp", lambda e, bi=bi, oc=oc: e.dma_start(
                        out=self.conv_spill[oc * 128:(oc + 1) * 128, :], in_=cvo[bi][:]),
                        reads=[f"cvo{bi}"], writes=["conv_spill"], dma=True)

            if self.debug:
                em.op("pool", lambda e: e.dma_start(out=self.dbg["xfmctx"], in_=self.xfm_ctx[:]),
                      reads=["xfm_ctx"], dma=True)
                em.op("sp", lambda e: e.dma_start(out=self.dbg["conv"], in_=self.conv_spill),
                      reads=["conv_spill"], dma=True)
                em.op("sp", lambda e: e.dma_start(out=self.dbg["dt"], in_=self.dt_hm[:]),
                      reads=["dt_hm"], dma=True)
            em.barrier()

    def phase2(self):
        em, nc = self.em, self.nc
        pb, PB = self.pb, self.PB
        self.p2stack = contextlib.ExitStack()
        ph = self.p2stack
        if True:
            def psb(name, shape, dt=F32):
                return ph.enter_context(nc.sbuf_tensor(name, list(shape), dt))

            xbB = [psb(f"xbB{i}", [128, 16 * 128]) for i in range(2)]
            x_tm = psb("x_tm", [128, 1024])
            b_tm = psb("b_tm", [128, 1024], BF16)
            dA = psb("dA", [64, 128])
            lrow = psb("lrow", [1, 384])
            self.lrow_ref = lrow
            dtmp = psb("dtmp", [128, 3 * 64])
            self.dtmp_ref = dtmp
            xdec = psb("xdec", [128, 512], BF16)
            gS = psb("gS", [64, 4096]); self.gS_ref = gS
            gS2 = psb("gS2", [64, 4096]); self.gS2_ref = gS2
            gl = psb("gl", [1, 64]); self.gl_ref = gl
            fmrow = psb("fmrow", [1, 16]); self.fmrow_ref = fmrow
            maskb = psb("maskb", [64, 8]); self.maskb_ref = maskb

            wz = [psb(f"wz{i}", [128, 512], BF16) for i in range(4)]
            zb = [psb(f"zb{i}", [128, 512]) for i in range(2)]

            def z_step_gen():
                for u in range(16):
                    tcg, zn = divmod(u, 8)
                    for k in range(16):
                        bi = k % 4
                        wk = f"wz{bi}"
                        em.op("pool", lambda e, bi=bi, k=k, zn=zn: e.dma_start(
                            out=wz[bi][:], in_=self.w_in_z[zn, k]),
                            writes=[wk], dma=True)
                        for tt in range(4):
                            tc = tcg * 4 + tt
                            em.op("pe", lambda e, bi=bi, k=k, tt=tt, tc=tc: e.matmul(
                                pb[2 + tt][:, 0:512],
                                self.x_fm[:, k * T + tc * 128:k * T + (tc + 1) * 128],
                                wz[bi][:], start=(k == 0), stop=(k == 15)),
                                reads=[wk, "x_fm"], writes=[PB[2 + tt]],
                                inc=(1 if tt == 3 else 0))
                        if k % 4 == 3 and k < 15:
                            yield
                    for tt in range(4):
                        tc = tcg * 4 + tt
                        zi = tt % 2
                        em.op("act", lambda e, tt=tt, zi=zi: e.activation(
                            zb[zi][:], pb[2 + tt][:, 0:512], AF.Silu),
                            reads=[PB[2 + tt]], writes=[f"zb{zi}"])
                        em.op("sp", lambda e, tc=tc, zn=zn, zi=zi: e.dma_start(
                            out=self.z_spill[tc * 128:(tc + 1) * 128,
                                             zn * 512:(zn + 1) * 512],
                            in_=zb[zi][:]),
                            reads=[f"zb{zi}"], writes=["z_spill"], dma=True)
                    yield

            zgen = z_step_gen() if self.upto >= 3 else iter(())
            zp = [0]

            def z_pump():
                if zp[0] >= 60:
                    return
                zp[0] += 1
                try:
                    next(zgen)
                except StopIteration:
                    pass

            em.op("sp", lambda e: e.dma_start(out=fmrow[:], in_=self.foldmask),
                  writes=["fmrow"], dma=True)
            em.op("pe", lambda e: e.matmul(pb[0][0:64, 0:8], self.cfv("onesrow")[0:1, 0:64],
                                           fmrow[0:1, 0:8], start=True, stop=True),
                  reads=["fmrow", "scf"], writes=[PB[0]])
            em.op("dve", lambda e: e.tensor_copy(maskb[:], pb[0][0:64, 0:8]),
                  reads=[PB[0]], writes=["maskb"])
            em.op("dve", lambda e: e.memset(self.prevs[0][:], 0.0), writes=["prevs0"])
            em.op("dve", lambda e: e.memset(self.sumlast[:], 0.0), writes=["sumlast"])

            for c in range(self.p2_chunks):
                bi = c % 2
                xk = f"xbB{bi}"
                em.op("sp", lambda e, bi=bi, c=c: e.dma_start(
                    out=xbB[bi][:].rearrange("p (kc n) -> p kc n", kc=16),
                    in_=self.conv_spill[0:2048, c * 128:(c + 1) * 128]
                        .rearrange("(kc p) n -> p kc n", p=128)),
                    writes=[xk], dma=True)
                for g8 in range(8 if self.p2_parts & 1 else 0):
                    bk = PB[g8 % 2]
                    em.op("pe", lambda e, g8=g8, bi=bi: e.transpose(
                        pb[g8 % 2][:, 0:128], xbB[bi][:, g8 * 128:(g8 + 1) * 128],
                        self.cfv("identf")),
                        reads=[xk, "scf"], writes=[bk])
                    em.op("dve", lambda e, g8=g8: e.tensor_copy(
                        x_tm[:, g8 * 128:(g8 + 1) * 128], pb[g8 % 2][:, 0:128]),
                        reads=[bk], writes=["x_tm"])
                for g8 in range(8 if self.p2_parts & 1 else 0):
                    bk = PB[g8 % 2]
                    em.op("pe", lambda e, g8=g8, bi=bi: e.transpose(
                        pb[g8 % 2][:, 0:128], xbB[bi][:, (8 + g8) * 128:(9 + g8) * 128],
                        self.cfv("identf")),
                        reads=[xk, "scf"], writes=[bk])
                    em.op("dve", lambda e, g8=g8: e.tensor_copy(
                        b_tm[:, g8 * 128:(g8 + 1) * 128], pb[g8 % 2][:, 0:128]),
                        reads=[bk], writes=["b_tm"])
                if not (self.p2_parts & 2):
                    continue
                _n = [0]
                _lim = getattr(self, "p2_fine", 99)
                _orig_op = em.op
                def _gated_op(*a, **k):
                    if _n[0] >= _lim:
                        _n[0] += 1
                        return
                    _n[0] += 1
                    return _orig_op(*a, **k)
                em.op = _gated_op
                em.op("dve", lambda e, c=c: e.tensor_scalar(
                    dA[:], self.dt_hm[:, c * 128:(c + 1) * 128], self.acol[0:64, 1:2],
                    None, ALU.mult),
                    reads=["dt_hm", "acol"], writes=["dA"])
                em.op("dve", lambda e, c=c: e.tensor_tensor_scan(
                    self.dAc_hm[:, c * 128:(c + 1) * 128], dA[:], dA[:], 0.0,
                    ALU.add, ALU.bypass),
                    reads=["dA"], writes=["dAc_hm"])
                em.op("pe", lambda e, c=c: e.transpose(
                    pb[0][:, 0:64], self.dAc_hm[0:64, c * 128:(c + 1) * 128],
                    self.cfv("identf")[0:64, 0:64]),
                    reads=["dAc_hm", "scf"], writes=[PB[0]])
                em.op("dve", lambda e, c=c: e.tensor_copy(
                    self.dAcT[:, c * 64:(c + 1) * 64], pb[0][:, 0:64]),
                    reads=[PB[0]], writes=["dAcT"])
                em.op("pe", lambda e, c=c: e.transpose(
                    pb[1][:, 0:64], self.dt_hm[0:64, c * 128:(c + 1) * 128],
                    self.cfv("identf")[0:64, 0:64]),
                    reads=["dt_hm", "scf"], writes=[PB[1]])
                em.op("dve", lambda e, c=c: e.tensor_copy(
                    self.dtT[:, c * 64:(c + 1) * 64], pb[1][:, 0:64]),
                    reads=[PB[1]], writes=["dtT"])
                em.op("dve", lambda e, c=c: e.tensor_scalar(
                    self.ndAcT[:, c * 64:(c + 1) * 64], self.dAcT[:, c * 64:(c + 1) * 64],
                    -1.0, None, ALU.mult),
                    reads=["dAcT"], writes=["ndAcT"])
                em.op("act", lambda e, c=c: e.activation(
                    self.edAcT[:, c * 64:(c + 1) * 64], self.dAcT[:, c * 64:(c + 1) * 64],
                    AF.Exp),
                    reads=["dAcT"], writes=["edAcT"])
                em.op("pe", lambda e, c=c: e.transpose(
                    pb[0][0:1, 0:64], self.dAc_hm[0:64, c * 128 + 127:c * 128 + 128],
                    self.cfv("identf")[0:64, 0:64]),
                    reads=["dAc_hm", "scf"], writes=[PB[0]])
                lo = (c % 2) * 192
                em.op("dve", lambda e, lo=lo: e.tensor_copy(lrow[0:1, lo:lo + 64], pb[0][0:1, 0:64]),
                      reads=[PB[0]], writes=["lrow"])
                em.op("dve", lambda e, lo=lo: e.tensor_add(
                    self.sumlast[0:1, 0:64], self.sumlast[0:1, 0:64], lrow[0:1, lo:lo + 64]),
                    reads=["lrow", "sumlast"], writes=["sumlast"])
                em.op("act", lambda e, lo=lo: e.activation(
                    lrow[0:1, lo + 64:lo + 128], lrow[0:1, lo:lo + 64], AF.Exp),
                    reads=["lrow"], writes=["lrow"])
                em.op("pe", lambda e, lo=lo: e.matmul(
                    pb[0][0:64, 0:64], self.cfv("onesrow")[0:1, 0:64],
                    lrow[0:1, lo + 64:lo + 128],
                    start=True, stop=True),
                    reads=["lrow", "scf"], writes=[PB[0]])
                em.op("dve", lambda e, c=c: e.tensor_copy(
                    self.cdb[:, c * 64:(c + 1) * 64], pb[0][0:64, 0:64]),
                    reads=[PB[0]], writes=["cdb"])
                em.op("pe", lambda e, lo=lo: e.matmul(
                    pb[1][:, 0:64], self.cfv("onesrow")[0:1, :], lrow[0:1, lo:lo + 64],
                    start=True, stop=True),
                    reads=["lrow", "scf"], writes=[PB[1]])
                em.op("dve", lambda e, c=c: e.tensor_sub(
                    dtmp[:, 0:64], pb[1][:, 0:64], self.dAcT[:, c * 64:(c + 1) * 64]),
                    reads=[PB[1], "dAcT"], writes=["dtmp"])
                em.op("act", lambda e: e.activation(dtmp[:, 64:128], dtmp[:, 0:64], AF.Exp),
                      reads=["dtmp"], writes=["dtmp"])
                em.op("dve", lambda e, c=c: e.tensor_mul(
                    self.decT[:, c * 64:(c + 1) * 64], self.dtT[:, c * 64:(c + 1) * 64],
                    dtmp[:, 64:128]),
                    reads=["dtmp", "dtT"], writes=["decT"])
                em.op = _orig_op
                z_pump()
                src = c % 2
                dst = 1 - src
                for w in range(8 if self.p2_parts & 4 else 0):
                    for hh in range(8):
                        h = w * 8 + hh
                        g = h // 4
                        em.op("dve", lambda e, h=h, hh=hh, g=g, c=c: e.tensor_scalar(
                            xdec[:, hh * 64:(hh + 1) * 64], x_tm[:, g * 64:(g + 1) * 64],
                            self.decT[:, c * 64 + h:c * 64 + h + 1], None, ALU.mult),
                            reads=["x_tm", "decT"], writes=["xdec"])
                    for hh in range(8):
                        h = w * 8 + hh
                        g = h // 4
                        em.op("pe", lambda e, h=h, hh=hh, g=g: e.matmul(
                            pb[0][0:64, hh * 64:(hh + 1) * 64],
                            b_tm[:, g * 64:(g + 1) * 64], xdec[:, hh * 64:(hh + 1) * 64],
                            start=True, stop=True),
                            reads=["b_tm", "xdec"], writes=[PB[0]],
                            inc=(1 if hh == 7 else 0))
                    for hh in range(8):
                        h = w * 8 + hh
                        em.op("dve", lambda e, h=h, hh=hh, c=c, src=src, dst=dst:
                              e.scalar_tensor_tensor(
                                  self.prevs[dst][:, h * 64:(h + 1) * 64],
                                  self.prevs[src][:, h * 64:(h + 1) * 64],
                                  self.cdb[0:64, c * 64 + h:c * 64 + h + 1],
                                  pb[0][0:64, hh * 64:(hh + 1) * 64],
                                  ALU.mult, ALU.add),
                              reads=[f"prevs{src}", "cdb", PB[0]],
                              writes=[f"prevs{dst}"])
                    z_pump()
            if not self.p2_tail:
                em.barrier()
                return
            final = NCH % 2
            em.op("pool", lambda e, final=final: e.dma_start(
                out=self.cc_in.ap()[0:64, :], in_=self.prevs[final][:]),
                reads=[f"prevs{final}"], writes=["cc_in"], dma=True)
            em.op("pool", lambda e: e.dma_start(
                out=self.cc_in.ap()[64:65, :].rearrange("one (p n) -> one p n", p=128),
                in_=self.cfv("zeros32")),
                reads=["scf"], writes=["cc_in"], dma=True)
            em.op("pool", lambda e: e.dma_start(
                out=self.cc_in.ap()[64:65, 0:64], in_=self.sumlast[0:1, 0:64]),
                reads=["sumlast"], writes=["cc_in"], dma=True)
            if self.n_cores > 1 and not getattr(self, "no_collective", False):
                em.op("pool", lambda e: e.collective_compute(
                    "AllGather", ALU.bypass, replica_groups=[list(range(self.n_cores))],
                    ins=[self.cc_in.ap().opt()], outs=[self.cc_out.ap().opt()]),
                    reads=["cc_in"], writes=["cc_out"], cc=True)
            else:
                em.op("pool", lambda e: e.dma_start(out=self.cc_out.ap()[0:65, :],
                                                    in_=self.cc_in.ap()),
                      reads=["cc_in"], writes=["cc_out"], dma=True)
            for _ in zgen:
                pass
            if self.upto >= 3 and self.debug:
                em.op("sp", lambda e: e.dma_start(out=self.dbg["z"], in_=self.z_spill),
                      reads=["z_spill"], dma=True)
            em.barrier(skip_cc=True)

    def phase2fold(self):
        em, nc = self.em, self.nc
        pb, PB = self.pb, self.PB
        gS_ = self.gS_ref; gS2_ = self.gS2_ref; gl_ = self.gl_ref
        fmrow_ = self.fmrow_ref; maskb_ = self.maskb_ref
        dtmp_ = self.dtmp_ref; lrow_ = self.lrow_ref
        if True:
            em.op("dve", lambda e: e.memset(self.prevs[0][:], 0.0),
                  writes=["prevs0"])
            em.op("dve", lambda e: e.memset(self.prevs[1][:], 0.0),
                  writes=["prevs1"])
            nfold = (min(7, self.n_cores - 1) if self.n_cores > 1 else 0)
            if getattr(self, "skip_fold", False):
                nfold = 0
            for m in range(nfold):
                src, dst = m % 2, 1 - m % 2
                em.op("pool", lambda e, m=m: e.dma_start(
                    out=gS_[:], in_=self.cc_out.ap()[m * 65:m * 65 + 64, :]),
                    reads=["cc_out"], writes=["gS"], dma=True)
                em.op("pool", lambda e, m=m: e.dma_start(
                    out=gl_[:], in_=self.cc_out.ap()[m * 65 + 64:m * 65 + 65, 0:64]),
                    reads=["cc_out"], writes=["gl"], dma=True)
                lo = (m % 2) * 192
                em.op("act", lambda e, lo=lo: e.activation(
                    lrow_[0:1, lo:lo + 64], gl_[0:1, :], AF.Exp),
                    reads=["gl"], writes=["lrow"])
                em.op("dve", lambda e, m=m, lo=lo: e.tensor_scalar(
                    lrow_[0:1, lo + 64:lo + 128], lrow_[0:1, lo:lo + 64], fmrow_[0:1, m:m + 1],
                    fmrow_[0:1, 8 + m:9 + m], ALU.mult, ALU.add),
                    reads=["lrow", "fmrow"], writes=["lrow"])
                em.op("pe", lambda e, lo=lo: e.matmul(
                    pb[4][0:64, 0:64], self.cfv("onesrow")[0:1, 0:64],
                    lrow_[0:1, lo + 64:lo + 128],
                    start=True, stop=True),
                    reads=["lrow", "scf"], writes=[PB[4]])
                em.op("dve", lambda e: e.tensor_copy(dtmp_[0:64, 128:192], pb[4][0:64, 0:64]),
                      reads=[PB[4]], writes=["dtmp"])
                em.op("dve", lambda e, m=m: e.tensor_scalar(
                    gS2_[:], gS_[:], maskb_[:, m:m + 1], None, ALU.mult),
                    reads=["gS", "maskb"], writes=["gS2"])
                for h in range(NH):
                    em.op("dve", lambda e, h=h, src=src, dst=dst: e.scalar_tensor_tensor(
                        self.prevs[dst][:, h * 64:(h + 1) * 64],
                        self.prevs[src][:, h * 64:(h + 1) * 64],
                        dtmp_[0:64, 128 + h:129 + h],
                        gS2_[:, h * 64:(h + 1) * 64],
                        ALU.mult, ALU.add),
                        reads=[f"prevs{src}", "dtmp", "gS2"],
                        writes=[f"prevs{dst}"])
            if self.debug:
                em.op("pool", lambda e: e.dma_start(
                    out=self.dbg["prevs"], in_=self.prevs[1][:]),
                    reads=["prevs1"], dma=True)
                em.op("pool", lambda e: e.dma_start(out=self.dbg["dacT"], in_=self.dAcT[:]),
                      reads=["dAcT"], dma=True)
                em.op("pool", lambda e: e.dma_start(out=self.dbg["edacT"], in_=self.edAcT[:]),
                      reads=["edAcT"], dma=True)
                em.op("pool", lambda e: e.dma_start(out=self.dbg["dachm"], in_=self.dAc_hm[:]),
                      reads=["dAc_hm"], dma=True)
            em.barrier()
        self.p2stack.close()

    def phase2z(self):
        em, nc = self.em, self.nc
        pb, PB = self.pb, self.PB
        with contextlib.ExitStack() as ph:
            def psb(name, shape, dt=F32):
                return ph.enter_context(nc.sbuf_tensor(name, list(shape), dt))

            wz = [psb(f"wz{i}", [128, 512], BF16) for i in range(2)]
            zb = [psb(f"zb{i}", [128, 512]) for i in range(2)]
            for tcg in range(2):
                for zn in range(8):
                    for k in range(16):
                        bi = k % 2
                        wk = f"wz{bi}"
                        em.op("pool", lambda e, bi=bi, k=k, zn=zn: e.dma_start(
                            out=wz[bi][:], in_=self.w_in_z[zn, k]),
                            writes=[wk], dma=True)
                        for tt in range(4):
                            tc = tcg * 4 + tt
                            em.op("pe", lambda e, bi=bi, k=k, tt=tt, tc=tc: e.matmul(
                                pb[tt][:, 0:512],
                                self.x_fm[:, k * T + tc * 128:k * T + (tc + 1) * 128],
                                wz[bi][:], start=(k == 0), stop=(k == 15)),
                                reads=[wk, "x_fm"], writes=[PB[tt]],
                                inc=(1 if tt == 3 else 0))
                    for tt in range(4):
                        tc = tcg * 4 + tt
                        zi = tt % 2
                        em.op("act", lambda e, tt=tt, zi=zi: e.activation(
                            zb[zi][:], pb[tt][:, 0:512], AF.Silu),
                            reads=[PB[tt]], writes=[f"zb{zi}"])
                        em.op("sp", lambda e, tc=tc, zn=zn, zi=zi: e.dma_start(
                            out=self.z_spill[tc * 128:(tc + 1) * 128,
                                             zn * 512:(zn + 1) * 512],
                            in_=zb[zi][:]),
                            reads=[f"zb{zi}"], writes=["z_spill"], dma=True)
            if self.debug:
                em.op("sp", lambda e: e.dma_start(out=self.dbg["z"], in_=self.z_spill),
                      reads=["z_spill"], dma=True)

    def phase3(self):
        em, nc = self.em, self.nc
        pb, pt, PB, PT = self.pb, self.pt, self.PB, self.PT
        self.xstack.close()
        with contextlib.ExitStack() as ph:
            def psb(name, shape, dt=F32):
                return ph.enter_context(nc.sbuf_tensor(name, list(shape), dt))

            bcf = psb("bcf", [64, 80 * 128], BF16)
            xbr = [psb(f"xbr{i}", [128, 8 * 128]) for i in range(2)]
            x_tm = psb("x_tm3", [128, 1024])
            b_tm = psb("b_tm3", [128, 1024], BF16)
            Lw = psb("Lw", [128, 1024])
            S1 = psb("S1", [128, 1024], BF16)
            xdt = psb("xdt", [128, 512], BF16)
            xdec = psb("xdec3", [128, 512], BF16)
            pcb = psb("pcb", [64, 4096], BF16)
            twave = psb("twave", [128, 512])
            uwave = psb("uwave", [128, 512])
            ygsl = psb("ygsl", [128, 512])
            siluz = [psb(f"siluz{i}", [128, 512]) for i in range(2)]
            ygs = psb("ygs", [128, 4096])
            ybf = psb("ybf", [128, 1024], BF16)
            yfm = psb("yfm", [128, 32 * 128], BF16)
            ncol = psb("ncol", [128, 10])
            wo32 = [psb(f"wo32_{i}", [128, 512]) for i in range(3)]
            wobf = [psb(f"wobf{i}", [128, 512], BF16) for i in range(3)]
            yk = [psb(f"yk{i}", [128, 512], BF16) for i in range(3)]
            a0r = [psb(f"a0r{i}", [128, 512]) for i in range(2)]
            pr1 = psb("pr1", [128, 512])
            r1sb = [psb(f"r1sb{i}", [128, 512]) for i in range(2)]

            em.op("sp", lambda e: e.dma_start(out=self.normw_sb[:], in_=self.norm_w),
                  writes=["normw_sb"], dma=True)

            for c in range(self.p3_chunks):
                bi = c % 2
                bck = "bcf"
                xbk = f"xbr{bi}"
                em.op("pool", lambda e, c=c: e.dma_start(
                    out=bcf[:].rearrange("p (kc n) -> p kc n", kc=80),
                    in_=self.conv_spill[1024:6144, c * 128:(c + 1) * 128]
                        .rearrange("(kc pl) n -> pl kc n", pl=64)),
                    writes=[bck], dma=True)
                em.op("sp", lambda e, bi=bi, c=c: e.dma_start(
                    out=xbr[bi][:].rearrange("p (kc n) -> p kc n", kc=8),
                    in_=self.conv_spill[0:1024, c * 128:(c + 1) * 128]
                        .rearrange("(kc p) n -> p kc n", p=128)),
                    writes=[xbk], dma=True)

                for g8 in range(8):
                    bk = PB[g8 % 2]
                    em.op("pe", lambda e, g8=g8, bi=bi: e.transpose(
                        pb[g8 % 2][:, 0:128], xbr[bi][:, g8 * 128:(g8 + 1) * 128],
                        self.cfv("identf")),
                        reads=[xbk, "scf"], writes=[bk])
                    em.op("dve", lambda e, g8=g8: e.tensor_copy(
                        x_tm[:, g8 * 128:(g8 + 1) * 128], pb[g8 % 2][:, 0:128]),
                        reads=[bk], writes=["x_tm3"])
                for g in range(16):
                    bk = PT[g % 2]
                    em.op("pe", lambda e, g=g: e.transpose(
                        pt[g % 2][:, 0:64], bcf[0:64, g * 128:(g + 1) * 128],
                        self.cbv("identb")[0:64, 0:64]),
                        reads=[bck, "scb"], writes=[bk])
                    em.op("dve", lambda e, g=g: e.tensor_copy(
                        b_tm[:, g * 64:(g + 1) * 64], pt[g % 2][:, 0:64]),
                        reads=[bk], writes=["b_tm3"])

                def cview(h):
                    return bcf[0:64, (16 + h) * 128:(17 + h) * 128]

                def bviewm(h):
                    g = h // 4
                    return bcf[0:64, g * 128:(g + 1) * 128]

                src = (c + 1) % 2
                dst = 1 - src
                em.op("dve", lambda e, src=src: e.tensor_copy(
                    pcb[:], self.prevs[src][:]),
                    reads=[f"prevs{src}"], writes=["pcb"])
                for w in range(8 if self.p3_parts & 1 else 0):
                    h0 = w * 8
                    em.op("sp", lambda e, c=c, w=w: e.dma_start(
                        out=siluz[w % 2][:],
                        in_=self.z_spill[c * 128:(c + 1) * 128, w * 512:(w + 1) * 512]),
                        writes=[f"siluz{w % 2}"], dma=True)
                    for hh in range(8):
                        h = h0 + hh
                        bk = PB[hh // 4]
                        em.op("pe", lambda e, h=h, hh=hh, c=c: e.matmul(
                            pb[hh // 4][:, (hh % 4) * 128:(hh % 4 + 1) * 128],
                            self.selv(h), self.dAc_hm[0:64, c * 128:(c + 1) * 128],
                            start=True, stop=False),
                            reads=["ssel", "dAc_hm"], writes=[bk], inc=0)
                        em.op("pe", lambda e, hh=hh: e.matmul(
                            pb[hh // 4][:, (hh % 4) * 128:(hh % 4 + 1) * 128],
                            self.cfv("identf"), self.cfv("trilneg"),
                            start=False, stop=True),
                            reads=["scf"], writes=[bk],
                            inc=(1 if hh % 4 == 3 else 0))
                    for hh in range(8):
                        h = h0 + hh
                        em.op("act", lambda e, h=h, hh=hh, c=c: e.activation(
                            Lw[:, hh * 128:(hh + 1) * 128],
                            pb[hh // 4][:, (hh % 4) * 128:(hh % 4 + 1) * 128],
                            AF.Exp, bias=self.ndAcT[:, c * 64 + h:c * 64 + h + 1]),
                            reads=[PB[hh // 4], "ndAcT"], writes=["Lw"])
                    for hh in range(8 if self.p3_parts & 2 else 0):
                        h = h0 + hh
                        g = h // 4
                        bk = PB[2 + hh // 4]
                        em.op("pe", lambda e, h=h, hh=hh, cview=cview, bviewm=bviewm:
                              e.matmul(
                                  pb[2 + hh // 4][:, (hh % 4) * 128:(hh % 4 + 1) * 128],
                                  bviewm(h), cview(h), start=True, stop=True),
                              reads=[bck], writes=[bk],
                              inc=(1 if hh % 4 == 3 else 0))
                    for half in range(2 if self.p3_parts & 256 else 0):
                        em.op("dve", lambda e, half=half: e.tensor_mul(
                            S1[:, half * 512:(half + 1) * 512],
                            Lw[:, half * 512:(half + 1) * 512],
                            pb[2 + half][:, 0:512]),
                            reads=["Lw", PB[2 + half]], writes=["S1"])
                    for hh in range(8 if self.p3_parts & 4 else 0):
                        h = h0 + hh
                        g = h // 4
                        em.op("dve", lambda e, h=h, hh=hh, g=g, c=c: e.tensor_scalar(
                            xdt[:, hh * 64:(hh + 1) * 64], x_tm[:, g * 64:(g + 1) * 64],
                            self.dtT[:, c * 64 + h:c * 64 + h + 1], None, ALU.mult),
                            reads=["x_tm3", "dtT"], writes=["xdt"])
                    for hh in range(8 if self.p3_parts & 4 else 0):
                        em.op("pe", lambda e, hh=hh: e.matmul(
                            pb[4][:, hh * 64:(hh + 1) * 64],
                            S1[:, hh * 128:(hh + 1) * 128], xdt[:, hh * 64:(hh + 1) * 64],
                            start=True, stop=True),
                            reads=["S1", "xdt"], writes=[PB[4]],
                            inc=(1 if hh == 7 else 0))
                    for hh in range(8 if self.p3_parts & 8 else 0):
                        h = h0 + hh
                        em.op("pe", lambda e, h=h, hh=hh, cview=cview: e.matmul(
                            pb[5][:, hh * 64:(hh + 1) * 64],
                            cview(h), pcb[0:64, h * 64:(h + 1) * 64],
                            start=True, stop=True),
                            reads=[bck, "pcb"], writes=[PB[5]],
                            inc=(1 if hh == 7 else 0))
                    for hh in range(8 if self.p3_parts & 16 else 0):
                        h = h0 + hh
                        g = h // 4
                        em.op("dve", lambda e, h=h, hh=hh, g=g, c=c: e.tensor_scalar(
                            xdec[:, hh * 64:(hh + 1) * 64], x_tm[:, g * 64:(g + 1) * 64],
                            self.decT[:, c * 64 + h:c * 64 + h + 1], None, ALU.mult),
                            reads=["x_tm3", "decT"], writes=["xdec3"])
                    for hh in range(8 if self.p3_parts & 16 else 0):
                        h = h0 + hh
                        g = h // 4
                        em.op("pe", lambda e, h=h, hh=hh, g=g: e.matmul(
                            pb[0][0:64, hh * 64:(hh + 1) * 64],
                            b_tm[:, g * 64:(g + 1) * 64], xdec[:, hh * 64:(hh + 1) * 64],
                            start=True, stop=True),
                            reads=["b_tm3", "xdec3"], writes=[PB[0]],
                            inc=(1 if hh == 7 else 0))
                    for hh in range(8 if self.p3_parts & 16 else 0):
                        h = h0 + hh
                        em.op("dve", lambda e, h=h, hh=hh, c=c, src=src, dst=dst:
                              e.scalar_tensor_tensor(
                                  self.prevs[dst][:, h * 64:(h + 1) * 64],
                                  self.prevs[src][:, h * 64:(h + 1) * 64],
                                  self.cdb[0:64, c * 64 + h:c * 64 + h + 1],
                                  pb[0][0:64, hh * 64:(hh + 1) * 64],
                                  ALU.mult, ALU.add),
                              reads=[f"prevs{src}", "cdb", PB[0]],
                              writes=[f"prevs{dst}"])
                    for hh in range(8 if self.p3_parts & 32 else 0):
                        h = h0 + hh
                        em.op("dve", lambda e, h=h, hh=hh, c=c: e.tensor_scalar(
                            twave[:, hh * 64:(hh + 1) * 64],
                            pb[5][:, hh * 64:(hh + 1) * 64],
                            self.edAcT[:, c * 64 + h:c * 64 + h + 1], None, ALU.mult),
                            reads=[PB[5], "edAcT"], writes=["twave"])
                    for hh in range(8 if self.p3_parts & 32 else 0):
                        h = h0 + hh
                        g = h // 4
                        em.op("dve", lambda e, h=h, hh=hh, g=g: e.scalar_tensor_tensor(
                            uwave[:, hh * 64:(hh + 1) * 64], x_tm[:, g * 64:(g + 1) * 64],
                            self.dbcast[:, h:h + 1], twave[:, hh * 64:(hh + 1) * 64],
                            ALU.mult, ALU.add),
                            reads=["x_tm3", "dbcast", "twave"], writes=["uwave"])
                    em.op("dve", lambda e, h0=h0: e.tensor_add(
                        ygsl[:], uwave[:], pb[4][:, 0:512]),
                        reads=["uwave", PB[4]], writes=["ygsl"])
                    em.op("dve", lambda e, h0=h0, w=w: e.tensor_mul(
                        ygs[:, h0 * 64:(h0 + 8) * 64], ygsl[:], siluz[w % 2][:]),
                        reads=["ygsl", f"siluz{w % 2}"], writes=["ygs"])

                if not (self.p3_parts & 64):
                    continue
                em.op("act", lambda e: e.activation(
                    Lw[:], ygs[:, 0:1024], AF.Square, accum_out=ncol[:, 0:1]),
                    reads=["ygs"], writes=["Lw", "ncol"])
                em.op("act", lambda e: e.activation(
                    Lw[:], ygs[:, 1024:2048], AF.Square, accum_out=ncol[:, 1:2]),
                    reads=["ygs"], writes=["Lw", "ncol"])
                em.op("act", lambda e: e.activation(
                    Lw[:], ygs[:, 2048:3072], AF.Square, accum_out=ncol[:, 2:3]),
                    reads=["ygs"], writes=["Lw", "ncol"])
                em.op("act", lambda e: e.activation(
                    Lw[:], ygs[:, 3072:4096], AF.Square, accum_out=ncol[:, 3:4]),
                    reads=["ygs"], writes=["Lw", "ncol"])
                em.op("dve", lambda e: e.tensor_add(
                    ncol[:, 4:5], ncol[:, 0:1], ncol[:, 1:2]), reads=["ncol"], writes=["ncol2"])
                em.op("dve", lambda e: e.tensor_add(
                    ncol[:, 5:6], ncol[:, 2:3], ncol[:, 3:4]), reads=["ncol"], writes=["ncol2"])
                em.op("dve", lambda e: e.tensor_add(
                    ncol[:, 6:7], ncol[:, 4:5], ncol[:, 5:6]),
                    reads=["ncol2"], writes=["ncol3"])
                em.op("dve", lambda e: e.tensor_scalar(
                    ncol[:, 7:8], ncol[:, 6:7], 1.0 / DI, None, ALU.mult),
                    reads=["ncol3"], writes=["ncol4"])
                em.op("act", lambda e: e.activation(
                    ncol[:, 8:9], ncol[:, 7:8], AF.Sqrt, bias=self.cfv("epscol")),
                    reads=["ncol4", "scf"], writes=["ncol5"])
                em.op("dve", lambda e: e.reciprocal(ncol[:, 9:10], ncol[:, 8:9]),
                      reads=["ncol5"], writes=["ncol6"])
                for q4 in range(4):
                    em.op("dve", lambda e, q4=q4: e.tensor_scalar(
                        ybf[:], ygs[:, q4 * 1024:(q4 + 1) * 1024], ncol[:, 9:10],
                        None, ALU.mult),
                        reads=["ygs", "ncol6"], writes=["ybf"])
                    for f8 in range(8):
                        fc = q4 * 8 + f8
                        bk = PT[f8 % 2]
                        em.op("pe", lambda e, f8=f8: e.transpose(
                            pt[f8 % 2][:, 0:128], ybf[:, f8 * 128:(f8 + 1) * 128],
                            self.cbv("identb")),
                            reads=["ybf", "scb"], writes=[bk])
                        em.op("dve", lambda e, fc=fc, f8=f8: e.tensor_copy(
                            yfm[:, fc * 128:(fc + 1) * 128], pt[f8 % 2][:, 0:128]),
                            reads=[bk], writes=["yfm"])
                em.op("sp", lambda e, c=c: e.dma_start(
                    out=self.yfm_spill[0:4096, c * 128:(c + 1) * 128]
                        .rearrange("(kc p) n -> p kc n", p=128),
                    in_=yfm[:].rearrange("p (kc n) -> p kc n", kc=32)),
                    reads=["yfm"], writes=["yfm_spill"], dma=True)

                if c in (3, 7) and (self.p3_parts & 128):
                    blk = c // 4
                    for on in range(4):
                        for k in range(32):
                            wi = k % 3
                            em.op("sp", lambda e, wi=wi, k=k, on=on: e.dma_start(
                                out=wo32[wi][:], in_=self.w_out_t[on, k]),
                                writes=[f"wo32_{wi}"], dma=True)
                            em.op("dve", lambda e, wi=wi, k=k: e.tensor_scalar(
                                wobf[wi][:], wo32[wi][:], self.normw_sb[:, k:k + 1],
                                None, ALU.mult),
                                reads=[f"wo32_{wi}", "normw_sb"], writes=[f"wobf{wi}"])
                            em.op("pool", lambda e, wi=wi, k=k, blk=blk: e.dma_start(
                                out=yk[wi][:], in_=self.yfm_spill[k * 128:(k + 1) * 128,
                                                                  blk * 512:(blk + 1) * 512]),
                                reads=["yfm_spill"], writes=[f"yk{wi}"], dma=True)
                            for tt in range(4):
                                em.op("pe", lambda e, wi=wi, k=k, tt=tt: e.matmul(
                                    pb[tt][:, 0:512], yk[wi][:, tt * 128:(tt + 1) * 128],
                                    wobf[wi][:], start=(k == 0), stop=(k == 31)),
                                    reads=[f"yk{wi}", f"wobf{wi}"], writes=[PB[tt]],
                                    inc=(1 if tt == 3 else 0))
                        for tt in range(4):
                            cg = blk * 4 + tt
                            ai = tt % 2
                            em.op("sp", lambda e, cg=cg, on=on, ai=ai: e.dma_start(
                                out=a0r[ai][:], in_=self.x_own[cg * 128:(cg + 1) * 128,
                                                               on * 512:(on + 1) * 512]),
                                writes=[f"a0r{ai}"], dma=True)
                            em.op("dve", lambda e, cg=cg, ai=ai: e.tensor_scalar(
                                pr1[:], a0r[ai][:], self.stat(cg, 0), self.stat(cg, 1),
                                ALU.subtract, ALU.mult),
                                reads=[f"a0r{ai}", "stats"], writes=["pr1"])
                            ri = tt % 2
                            em.op("dve", lambda e, tt=tt, ri=ri: e.tensor_add(
                                r1sb[ri][:], pr1[:], pb[tt][:, 0:512]),
                                reads=["pr1", PB[tt]], writes=[f"r1sb{ri}"])
                            em.op("sp", lambda e, cg=cg, on=on, ri=ri: e.dma_start(
                                out=self.r1_spill[cg * 128:(cg + 1) * 128,
                                                  on * 512:(on + 1) * 512],
                                in_=r1sb[ri][:]),
                                reads=[f"r1sb{ri}"], writes=["r1_spill"], dma=True)

            if self.debug:
                em.op("sp", lambda e: e.dma_start(out=self.dbg["r1"], in_=self.r1_spill),
                      reads=["r1_spill"], dma=True)
                em.op("sp", lambda e: e.dma_start(out=self.dbg["y"], in_=self.yfm_spill),
                      reads=["yfm_spill"], dma=True)
            em.barrier()


    def phase5(self):
        em, nc = self.em, self.nc
        pb, pt, PB, PT = self.pb, self.pt, self.PB, self.PT
        self.ssdstack.close()
        with contextlib.ExitStack() as ph:
            def psb(name, shape, dt=F32):
                return ph.enter_context(nc.sbuf_tensor(name, list(shape), dt))

            hm = psb("hm", [128, 64 * 512], BF16)
            h2fm = psb("h2fm", [128, 16 * 512], BF16)
            r1r = [psb(f"r1r{i}", [128, DM]) for i in range(2)]
            h2b = psb("h2b", [128, DM], BF16)
            h2f = psb("h2f", [128, DM])
            sqs5 = psb("sqs5", [128, 1024])
            scc5 = psb("scc5", [128, 8])
            wgb = [psb(f"wgb{i}", [128, 16 * 128], BF16) for i in range(3)]
            wub = [psb(f"wub{i}", [128, 16 * 128], BF16) for i in range(3)]
            wdb = [psb(f"wdb{i}", [128, 1024], BF16) for i in range(4)]
            sgb = psb("sgb", [128, 512])
            a0r5 = [psb(f"a0r5{i}", [128, 512]) for i in range(4)]
            r1s = [psb(f"r1s{i}", [128, 512]) for i in range(4)]
            a1s = [psb(f"a1s{i}", [128, 512]) for i in range(2)]
            t5 = psb("t5", [128, 512])

            for blk in range(2):
                for tc in range(4):
                    cg = blk * 4 + tc
                    ri = tc % 2
                    em.op("sp", lambda e, cg=cg, ri=ri: e.dma_start(
                        out=r1r[ri][:], in_=self.r1_spill[cg * 128:(cg + 1) * 128, :]),
                        writes=[f"r1r{ri}"], dma=True)
                    sq = scc5[:, 0:1]; ms = scc5[:, 1:2]; sqr = scc5[:, 2:3]; rst = scc5[:, 3:4]
                    em.op("act", lambda e, ri=ri, sq=sq: e.activation(
                        sqs5[:], r1r[ri][:, 0:1024], AF.Square, accum_out=sq),
                        reads=[f"r1r{ri}"], writes=["sqs5", "scc5a"])
                    em.op("act", lambda e, ri=ri: e.activation(
                        sqs5[:], r1r[ri][:, 1024:2048], AF.Square, accum_out=scc5[:, 4:5]),
                        reads=[f"r1r{ri}"], writes=["sqs5", "scc5b"])
                    em.op("dve", lambda e, sq=sq, ms=ms: e.tensor_add(ms, sq, scc5[:, 4:5]),
                          reads=["scc5a", "scc5b"], writes=["scc5c"])
                    em.op("dve", lambda e, ms=ms, sqr=sqr: e.tensor_scalar(
                        sqr, ms, 1.0 / DM, None, ALU.mult), reads=["scc5c"], writes=["scc5d"])
                    em.op("act", lambda e, sqr=sqr: e.activation(
                        scc5[:, 5:6], sqr, AF.Sqrt, bias=self.cfv("epscol")),
                        reads=["scc5d", "scf"], writes=["scc5e"])
                    em.op("dve", lambda e, rst=rst: e.reciprocal(rst, scc5[:, 5:6]),
                          reads=["scc5e"], writes=["scc5f"])
                    em.op("dve", lambda e, ri=ri, rst=rst: e.tensor_scalar(
                        h2f[:], r1r[ri][:], rst, None, ALU.mult),
                        reads=[f"r1r{ri}", "scc5f"], writes=["h2f"])
                    em.op("dve", lambda e: e.tensor_mul(h2b[:], h2f[:], self.lnw_b[:, 0:DM]),
                          reads=["h2f", "lnw_b"], writes=["h2b"])
                    for fc in range(16):
                        bk = PT[fc % 2]
                        em.op("pe", lambda e, fc=fc: e.transpose(
                            pt[fc % 2][:, 0:128], h2b[:, fc * 128:(fc + 1) * 128],
                            self.cbv("identb")),
                            reads=["h2b", "scb"], writes=[bk])
                        em.op("dve", lambda e, fc=fc, tc=tc: e.tensor_copy(
                            h2fm[:, fc * 512 + tc * 128: fc * 512 + (tc + 1) * 128],
                            pt[fc % 2][:, 0:128]),
                            reads=[bk], writes=["h2fm"])
                for oc in range(64):
                    wi = oc % 3
                    gk, uk = f"wgb{wi}", f"wub{wi}"
                    em.op("pool", lambda e, wi=wi, oc=oc: e.dma_start(
                        out=wgb[wi][:], in_=self.w_gate_t[oc]),
                        writes=[gk], dma=True)
                    em.op("pool", lambda e, wi=wi, oc=oc: e.dma_start(
                        out=wub[wi][:], in_=self.w_up_t[oc]),
                        writes=[uk], dma=True)
                    bg, bu = 2 * wi, 2 * wi + 1
                    for k in range(16):
                        em.op("pe", lambda e, wi=wi, k=k, bg=bg: e.matmul(
                            pb[bg][:, 0:512], wgb[wi][:, k * 128:(k + 1) * 128],
                            h2fm[:, k * 512:(k + 1) * 512],
                            start=(k == 0), stop=(k == 15)),
                            reads=[gk, "h2fm"], writes=[PB[bg]],
                            inc=(1 if k == 15 else 0))
                        em.op("pe", lambda e, wi=wi, k=k, bu=bu: e.matmul(
                            pb[bu][:, 0:512], wub[wi][:, k * 128:(k + 1) * 128],
                            h2fm[:, k * 512:(k + 1) * 512],
                            start=(k == 0), stop=(k == 15)),
                            reads=[uk, "h2fm"], writes=[PB[bu]],
                            inc=(1 if k == 15 else 0))
                    em.op("act", lambda e, bg=bg: e.activation(sgb[:], pb[bg][:, 0:512], AF.Silu),
                          reads=[PB[bg]], writes=["sgb"])
                    em.op("dve", lambda e, bu=bu, oc=oc: e.tensor_mul(
                        hm[:, oc * 512:(oc + 1) * 512], sgb[:], pb[bu][:, 0:512]),
                        reads=["sgb", PB[bu]], writes=["hm"])
                for on in range(4):
                    for tc in range(4):
                        cg = blk * 4 + tc
                        em.op("sp", lambda e, cg=cg, on=on, tc=tc: e.dma_start(
                            out=a0r5[tc][:], in_=self.x_own[cg * 128:(cg + 1) * 128,
                                                            on * 512:(on + 1) * 512]),
                            writes=[f"a0r5{tc}"], dma=True)
                        em.op("sp", lambda e, cg=cg, on=on, tc=tc: e.dma_start(
                            out=r1s[tc][:], in_=self.r1_spill[cg * 128:(cg + 1) * 128,
                                                              on * 512:(on + 1) * 512]),
                            writes=[f"r1s{tc}"], dma=True)
                    for hp in range(32):
                        wi = hp % 4
                        dk = f"wdb{wi}"
                        em.op("pool", lambda e, wi=wi, hp=hp, on=on: e.dma_start(
                            out=wdb[wi][:], in_=self.w_down_t[on, hp]),
                            writes=[dk], dma=True)
                        for half in range(2):
                            hk = 2 * hp + half
                            for tc in range(4):
                                em.op("pe", lambda e, wi=wi, hk=hk, tc=tc, half=half:
                                      e.matmul(
                                          pb[tc][:, 0:512],
                                          hm[:, hk * 512 + tc * 128:
                                             hk * 512 + (tc + 1) * 128],
                                          wdb[wi][:, half * 512:(half + 1) * 512],
                                          start=(hk == 0), stop=(hk == 63)),
                                      reads=[dk, "hm"], writes=[PB[tc]],
                                      inc=(1 if tc == 3 else 0))
                    for tc in range(4):
                        cg = blk * 4 + tc
                        bi2 = tc % 2
                        em.op("dve", lambda e, tc=tc: e.tensor_add(
                            t5[:], a0r5[tc][:], r1s[tc][:]),
                            reads=[f"a0r5{tc}", f"r1s{tc}"], writes=["t5"])
                        em.op("dve", lambda e, tc=tc, bi2=bi2: e.tensor_add(
                            a1s[bi2][:], t5[:], pb[tc][:, 0:512]),
                            reads=["t5", PB[tc]], writes=[f"a1s{bi2}"])
                        em.op("sp", lambda e, cg=cg, on=on, bi2=bi2: e.dma_start(
                            out=self.a1_spill[cg * 128:(cg + 1) * 128,
                                              on * 512:(on + 1) * 512],
                            in_=a1s[bi2][:]),
                            reads=[f"a1s{bi2}"], writes=["a1_spill"], dma=True)
            if self.debug:
                em.op("sp", lambda e: e.dma_start(out=self.dbg["a1"], in_=self.a1_spill),
                      reads=["a1_spill"], dma=True)
            em.barrier()

    def phase6(self):
        em, nc = self.em, self.nc
        pb, pt, PB, PT = self.pb, self.pt, self.PB, self.PT
        with contextlib.ExitStack() as ph:
            def psb(name, shape, dt=F32):
                return ph.enter_context(nc.sbuf_tensor(name, list(shape), dt))

            hall = psb("hall", [128, 88 * 512], BF16)
            x2fm = psb("x2fm", [128, 16 * 512], BF16)
            x2ffm = psb("x2ffm", [128, 16 * 128])
            a1r = [psb(f"a1r{i}", [128, DM]) for i in range(1)]
            x2b = psb("x2b", [128, DM], BF16)
            x2f6 = psb("x2f6", [128, DM])
            sqs6 = psb("sqs6", [128, 1024])
            scc6 = psb("scc6", [128, 10])
            swr = psb("swr", [128, 16 * 8])
            rcol = psb("rcol", [128, 24])
            rsc = psb("rsc", [128, 32])
            cfmr = psb("cfmr", [8, 512])
            wgb6 = [psb(f"wgb6{i}", [128, 16 * 128], BF16) for i in range(3)]
            wub6 = [psb(f"wub6{i}", [128, 16 * 128], BF16) for i in range(3)]
            wdb6 = [psb(f"wdb6{i}", [128, 1024], BF16) for i in range(4)]
            sgb6 = psb("sgb6", [128, 512])
            hbuf = psb("hbuf", [128, 512])
            cb6 = psb("cb6", [128, 512])
            a1s6 = [psb(f"a1s6{i}", [128, 512]) for i in range(4)]
            outs6 = [psb(f"outs6{i}", [128, 512]) for i in range(1)]

            em.op("sp", lambda e: e.dma_start(out=swr[:], in_=self.wr),
                  writes=["swr"], dma=True)

            for blk in range(2):
                for tc in range(4):
                    cg = blk * 4 + tc
                    ri = 0
                    em.op("sp", lambda e, cg=cg, ri=ri: e.dma_start(
                        out=a1r[ri][:], in_=self.a1_spill[cg * 128:(cg + 1) * 128, :]),
                        writes=[f"a1r{ri}"], dma=True)
                    sm = scc6[:, 0:1]; sq = scc6[:, 1:2]; mc = scc6[:, 2:3]
                    vc = scc6[:, 3:4]; tmp = scc6[:, 4:5]; r1c = scc6[:, 5:6]
                    em.op("act", lambda e, ri=ri, sm=sm: e.activation(
                        sqs6[:], a1r[ri][:, 0:1024], AF.Copy, accum_out=sm),
                        reads=[f"a1r{ri}"], writes=["sqs6", "scc6a"])
                    em.op("act", lambda e, ri=ri: e.activation(
                        sqs6[:], a1r[ri][:, 1024:2048], AF.Copy, accum_out=scc6[:, 6:7]),
                        reads=[f"a1r{ri}"], writes=["sqs6", "scc6b"])
                    em.op("act", lambda e, ri=ri, sq=sq: e.activation(
                        sqs6[:], a1r[ri][:, 0:1024], AF.Square, accum_out=sq),
                        reads=[f"a1r{ri}"], writes=["sqs6", "scc6c"])
                    em.op("act", lambda e, ri=ri: e.activation(
                        sqs6[:], a1r[ri][:, 1024:2048], AF.Square, accum_out=scc6[:, 7:8]),
                        reads=[f"a1r{ri}"], writes=["sqs6", "scc6d"])
                    em.op("dve", lambda e, sm=sm: e.tensor_add(scc6[:, 8:9], sm, scc6[:, 6:7]),
                          reads=["scc6a", "scc6b"], writes=["scc6e"])
                    em.op("dve", lambda e, sq=sq: e.tensor_add(scc6[:, 9:10], sq, scc6[:, 7:8]),
                          reads=["scc6c", "scc6d"], writes=["scc6f"])
                    em.op("dve", lambda e, mc=mc: e.tensor_scalar(
                        mc, scc6[:, 8:9], 1.0 / DM, None, ALU.mult),
                        reads=["scc6e"], writes=["scc6g"])
                    em.op("dve", lambda e, mc=mc, tmp=tmp: e.tensor_mul(tmp, mc, mc),
                          reads=["scc6g"], writes=["scc6h"])
                    em.op("dve", lambda e, vc=vc, tmp=tmp: e.scalar_tensor_tensor(
                        vc, scc6[:, 9:10], 1.0 / DM, tmp, ALU.mult, ALU.subtract),
                        reads=["scc6f", "scc6h"], writes=["scc6i"])
                    em.op("act", lambda e, vc=vc, tmp=tmp: e.activation(
                        scc6[:, 4:5], vc, AF.Sqrt, bias=self.cfv("epscol")),
                        reads=["scc6i", "scf"], writes=["scc6j"])
                    em.op("dve", lambda e, r1c=r1c: e.reciprocal(r1c, scc6[:, 4:5]),
                          reads=["scc6j"], writes=["scc6k"])
                    em.op("dve", lambda e, ri=ri, mc=mc, r1c=r1c: e.tensor_scalar(
                        x2f6[:], a1r[ri][:], mc, r1c, ALU.subtract, ALU.mult),
                        reads=[f"a1r{ri}", "scc6g", "scc6k"], writes=["x2f6"])
                    em.op("dve", lambda e: e.tensor_copy(x2b[:], x2f6[:]),
                          reads=["x2f6"], writes=["x2b"])
                    for fc in range(16):
                        bk = PT[fc % 2]
                        em.op("pe", lambda e, fc=fc: e.transpose(
                            pt[fc % 2][:, 0:128], x2b[:, fc * 128:(fc + 1) * 128],
                            self.cbv("identb")),
                            reads=["x2b", "scb"], writes=[bk])
                        em.op("dve", lambda e, fc=fc, tc=tc: e.tensor_copy(
                            x2fm[:, fc * 512 + tc * 128: fc * 512 + (tc + 1) * 128],
                            pt[fc % 2][:, 0:128]),
                            reads=[bk], writes=["x2fm"])
                    for fc in range(16):
                        bk = PB[fc % 2]
                        em.op("pe", lambda e, fc=fc: e.transpose(
                            pb[fc % 2][:, 0:128], x2f6[:, fc * 128:(fc + 1) * 128],
                            self.cfv("identf")),
                            reads=["x2f6", "scf"], writes=[bk])
                        em.op("dve", lambda e, fc=fc: e.tensor_copy(
                            x2ffm[:, fc * 128:(fc + 1) * 128], pb[fc % 2][:, 0:128]),
                            reads=[bk], writes=["x2ffm"])
                    for k in range(16):
                        em.op("pe", lambda e, k=k: e.matmul(
                            pb[5][:, 0:8], x2ffm[:, k * 128:(k + 1) * 128],
                            swr[:, k * 8:(k + 1) * 8], start=(k == 0), stop=(k == 15)),
                            reads=["x2ffm", "swr"], writes=[PB[5]],
                            inc=(1 if k == 15 else 0))
                    m1 = rcol[:, 0:1]; nm1 = rcol[:, 1:2]; ge1 = rcol[:, 2:10]
                    msk = rcol[:, 10:18]; m2 = rcol[:, 18:19]
                    den = rcol[:, 19:20]; rden = rcol[:, 20:21]
                    em.op("dve", lambda e, nm1=nm1: e.reduce_max(
                        nm1, pb[5][:, 0:8], axis=AX.X, negate=True),
                        reads=[PB[5]], writes=["rcA"])
                    em.op("dve", lambda e, m1=m1, nm1=nm1: e.tensor_scalar(
                        m1, nm1, -1.0, None, ALU.mult), reads=["rcA"], writes=["rcB"])
                    em.op("dve", lambda e, m1=m1, ge1=ge1: e.tensor_scalar(
                        ge1, pb[5][:, 0:8], m1, None, ALU.is_ge),
                        reads=[PB[5], "rcB"], writes=["rcC"])
                    em.op("dve", lambda e, ge1=ge1, msk=msk: e.scalar_tensor_tensor(
                        msk, ge1, -1e30, pb[5][:, 0:8], ALU.mult, ALU.add),
                        reads=["rcC", PB[5]], writes=["rcD"])
                    em.op("dve", lambda e, msk=msk, m2=m2: e.reduce_max(
                        m2, msk, axis=AX.X), reads=["rcD"], writes=["rcE"])
                    em.op("act", lambda e, nm1=nm1: e.activation(
                        rsc[:, 0:8], pb[5][:, 0:8], AF.Exp, bias=nm1),
                        reads=[PB[5], "rcA"], writes=["rcF"])
                    em.op("dve", lambda e, m2=m2: e.tensor_scalar(
                        rsc[:, 8:16], pb[5][:, 0:8], m2, None, ALU.is_ge),
                        reads=[PB[5], "rcE"], writes=["rcG"])
                    em.op("dve", lambda e: e.tensor_mul(
                        rsc[:, 16:24], rsc[:, 0:8], rsc[:, 8:16]),
                        reads=["rcF", "rcG"], writes=["rcH"])
                    em.op("dve", lambda e, den=den: e.reduce_sum(
                        den, rsc[:, 16:24], axis=AX.X), reads=["rcH"], writes=["rcI"])
                    em.op("dve", lambda e, den=den, rden=rden: e.reciprocal(rden, den),
                          reads=["rcI"], writes=["rcJ"])
                    em.op("dve", lambda e, rden=rden: e.tensor_scalar(
                        rsc[:, 24:32], rsc[:, 16:24], rden, None, ALU.mult),
                        reads=["rcH", "rcJ"], writes=["rcK"])
                    em.op("pe", lambda e: e.transpose(
                        pb[4][0:8, 0:128], rsc[:, 24:32], self.cfv("identf")),
                        reads=["rcK", "scf"], writes=[PB[4]])
                    em.op("dve", lambda e, tc=tc: e.tensor_copy(
                        cfmr[0:8, tc * 128:(tc + 1) * 128], pb[4][0:8, 0:128]),
                        reads=[PB[4]], writes=["cfmr"])

                for e_ in range(NEXP):
                    em.op("pe", lambda e, e_=e_: e.matmul(
                        pb[4][:, 0:512], self.sel8v(e_), cfmr[0:8, :],
                        start=True, stop=True),
                        reads=["scf", "cfmr"], writes=[PB[4]])
                    em.op("act", lambda e: e.activation(
                        cb6[:], pb[4][:, 0:512], AF.Copy),
                        reads=[PB[4]], writes=["cb6"])
                    for oc in range(11):
                        wi = oc % 3
                        gk, uk = f"wgb6{wi}", f"wub6{wi}"
                        em.op("pool", lambda e, wi=wi, oc=oc, e_=e_: e.dma_start(
                            out=wgb6[wi][:], in_=self.wg_t[e_, oc]),
                            writes=[gk], dma=True)
                        em.op("pool", lambda e, wi=wi, oc=oc, e_=e_: e.dma_start(
                            out=wub6[wi][:], in_=self.wu_t[e_, oc]),
                            writes=[uk], dma=True)
                        bg, bu = 2 * wi, 2 * wi + 1
                        for k in range(16):
                            em.op("pe", lambda e, wi=wi, k=k, bg=bg: e.matmul(
                                pb[bg][:, 0:512], wgb6[wi][:, k * 128:(k + 1) * 128],
                                x2fm[:, k * 512:(k + 1) * 512],
                                start=(k == 0), stop=(k == 15)),
                                reads=[gk, "x2fm"], writes=[PB[bg]],
                                inc=(1 if k == 15 else 0))
                            em.op("pe", lambda e, wi=wi, k=k, bu=bu: e.matmul(
                                pb[bu][:, 0:512], wub6[wi][:, k * 128:(k + 1) * 128],
                                x2fm[:, k * 512:(k + 1) * 512],
                                start=(k == 0), stop=(k == 15)),
                                reads=[uk, "x2fm"], writes=[PB[bu]],
                                inc=(1 if k == 15 else 0))
                        em.op("act", lambda e, bg=bg: e.activation(
                            sgb6[:], pb[bg][:, 0:512], AF.Silu),
                            reads=[PB[bg]], writes=["sgb6"])
                        em.op("dve", lambda e, bu=bu: e.tensor_mul(
                            hbuf[:], sgb6[:], pb[bu][:, 0:512]),
                            reads=["sgb6", PB[bu]], writes=["hbuf"])
                        em.op("dve", lambda e, e_=e_, oc=oc: e.tensor_mul(
                            hall[:, (e_ * 11 + oc) * 512:(e_ * 11 + oc + 1) * 512],
                            hbuf[:], cb6[:]),
                            reads=["hbuf", "cb6"], writes=["hall"])
                for on in range(4):
                    for tc in range(4):
                        cg = blk * 4 + tc
                        em.op("sp", lambda e, cg=cg, on=on, tc=tc: e.dma_start(
                            out=a1s6[tc][:],
                            in_=self.a1_spill[cg * 128:(cg + 1) * 128,
                                              on * 512:(on + 1) * 512]),
                            writes=[f"a1s6{tc}"], dma=True)
                    for ep in range(44):
                        wi = ep % 4
                        dk = f"wdb6{wi}"
                        em.op("pool", lambda e, wi=wi, ep=ep, on=on: e.dma_start(
                            out=wdb6[wi][:], in_=self.wd_t[on, ep]),
                            writes=[dk], dma=True)
                        for half in range(2):
                            ek = 2 * ep + half
                            for tc in range(4):
                                em.op("pe", lambda e, wi=wi, ek=ek, tc=tc, half=half:
                                      e.matmul(
                                          pb[tc][:, 0:512],
                                          hall[:, ek * 512 + tc * 128:
                                               ek * 512 + (tc + 1) * 128],
                                          wdb6[wi][:, half * 512:(half + 1) * 512],
                                          start=(ek == 0), stop=(ek == 87)),
                                      reads=[dk, "hall"], writes=[PB[tc]],
                                      inc=(1 if tc == 3 else 0))
                    for tc in range(4):
                        cg = blk * 4 + tc
                        bi2 = 0
                        em.op("dve", lambda e, tc=tc, bi2=bi2: e.tensor_add(
                            outs6[bi2][:], a1s6[tc][:], pb[tc][:, 0:512]),
                            reads=[f"a1s6{tc}", PB[tc]], writes=[f"outs6{bi2}"])
                        em.op("sp", lambda e, cg=cg, on=on, bi2=bi2: e.dma_start(
                            out=self.out[cg * 128:(cg + 1) * 128,
                                         on * 512:(on + 1) * 512],
                            in_=outs6[bi2][:]),
                            reads=[f"outs6{bi2}"], writes=["out"], dma=True)
            em.barrier()

    def finish(self):
        em = self.em
        if self.upto < 6:
            with contextlib.ExitStack() as ph:
                ob = ph.enter_context(self.nc.sbuf_tensor("ob", [128, DM], F32))
                em.op("dve", lambda e: e.memset(ob[:], 0.0), writes=["ob"])
                for c in range(NCH):
                    em.op("sp", lambda e, c=c: e.dma_start(
                        out=self.out[c * 128:(c + 1) * 128, :], in_=ob[:]),
                        reads=["ob"], writes=["out"], dma=True)
        em.barrier()

        em2 = self.em
        with self.nc.Block() as block:
            @block.tensor
            def _(eng):
                for f in em2.ops["pe"]:
                    f(eng)

            @block.vector
            def _(eng):
                for f in em2.ops["dve"]:
                    f(eng)

            @block.scalar
            def _(eng):
                for f in em2.ops["act"]:
                    f(eng)

            @block.gpsimd
            def _(eng):
                for f in em2.ops["pool"]:
                    f(eng)

            @block.sync
            def _(eng):
                for f in em2.ops["sp"]:
                    f(eng)


def build_sel():
    sel = np.zeros((128, 64 * 128), np.float32)
    for j in range(64):
        sel[j, j * 128:(j + 1) * 128] = 1.0
    return sel


def make_in_maps(inputs, debug=False):
    CF, _, CB, _ = build_consts()
    SEL = build_sel()
    ac = np.ascontiguousarray
    w_in = np.asarray(inputs["L0_attn_in_proj_w"])
    w4 = w_in.reshape(16, 128, 10304)
    W_IN_P1 = ac(w4[:, :, 4096:10240].reshape(16, 128, 48, 128)
                 .transpose(2, 1, 0, 3).reshape(48, 128, 2048))
    W_IN_DT = ac(w4[:, :, 10240:10304].transpose(1, 0, 2).reshape(128, 1024))
    W_IN_Z = ac(w4[:, :, 0:4096].reshape(16, 128, 8, 512).transpose(2, 0, 1, 3))
    wo = np.asarray(inputs["L0_attn_out_proj_w"])
    W_OUT_T = ac(wo.reshape(32, 128, 4, 512).transpose(2, 0, 1, 3))
    wgm = np.asarray(inputs["L0_attn_gate_w"])
    W_GATE_T = ac(wgm.reshape(16, 128, 64, 128).transpose(2, 1, 0, 3).reshape(64, 128, 2048))
    wum = np.asarray(inputs["L0_attn_up_w"])
    W_UP_T = ac(wum.reshape(16, 128, 64, 128).transpose(2, 1, 0, 3).reshape(64, 128, 2048))
    wdm = np.asarray(inputs["L0_attn_down_w"])
    W_DOWN_T = ac(wdm.reshape(64, 128, 4, 512).transpose(2, 0, 1, 3)
                  .reshape(4, 32, 2, 128, 512).transpose(0, 1, 3, 2, 4)
                  .reshape(4, 32, 128, 1024))
    wge = np.asarray(inputs["L0_ffn_Wg"])
    WG_T = ac(wge.reshape(8, 16, 128, 11, 128).transpose(0, 3, 2, 1, 4).reshape(8, 11, 128, 2048))
    wue = np.asarray(inputs["L0_ffn_Wu"])
    WU_T = ac(wue.reshape(8, 16, 128, 11, 128).transpose(0, 3, 2, 1, 4).reshape(8, 11, 128, 2048))
    wde = np.asarray(inputs["L0_ffn_Wd"])
    WD_T = ac(wde.reshape(8, 11, 128, 4, 512).transpose(3, 0, 1, 2, 4)
              .reshape(4, 44, 2, 128, 512).transpose(0, 1, 3, 2, 4)
              .reshape(4, 44, 128, 1024))
    wrm = np.asarray(inputs["L0_ffn_Wr"])
    WR_T = ac(wrm.reshape(16, 128, 8).transpose(1, 0, 2).reshape(128, 128))
    x0 = np.asarray(inputs["L0_attn_hidden_states"])[0]
    maps = []
    for k in range(NCORES):
        x_own = np.ascontiguousarray(x0[k * T:(k + 1) * T])
        if k == 0:
            x_ctx = np.zeros((128, DM), np.float32)
        else:
            x_ctx = np.ascontiguousarray(x0[k * T - 128:k * T])
        fm = np.zeros((1, 16), np.float32)
        fm[0, :k] = 1.0
        fm[0, 8:] = 1.0 - fm[0, :8]
        m = {
            "x_own": x_own,
            "x_ctx": x_ctx,
            "w_in_p1": W_IN_P1,
            "w_in_dt": W_IN_DT,
            "w_in_z": W_IN_Z,
            "conv_w": np.ascontiguousarray(
                np.asarray(inputs["L0_attn_conv_w"]).reshape(48, 128, DCONV)
                .transpose(1, 0, 2).reshape(128, 48 * DCONV)),
            "conv_b": np.ascontiguousarray(
                np.asarray(inputs["L0_attn_conv_b"]).reshape(48, 128).T),
            "dt_bias": np.asarray(inputs["L0_attn_dt_bias"]).reshape(NH, 1),
            "a_log": np.asarray(inputs["L0_attn_A_log"]).reshape(NH, 1),
            "d_vec": np.asarray(inputs["L0_attn_D"]).reshape(1, NH),
            "norm_w": np.ascontiguousarray(
                np.asarray(inputs["L0_attn_norm_w"]).reshape(32, 128).T),
            "w_out_t": W_OUT_T,
            "ln1_w": np.asarray(inputs["L0_attn_ln1_w"]).reshape(1, DM),
            "ln2_w": np.asarray(inputs["L0_attn_ln2_w"]).reshape(1, DM),
            "w_gate_t": W_GATE_T,
            "w_up_t": W_UP_T,
            "w_down_t": W_DOWN_T,
            "wr": WR_T,
            "wg_t": WG_T,
            "wu_t": WU_T,
            "wd_t": WD_T,
            "constf": CF,
            "constb": CB,
            "constsel": SEL,
            "foldmask": fm,
        }
        maps.append(m)
    return maps


def kernel(**inputs):
    b = Builder(debug=False)
    nc = b.build()
    maps = make_in_maps(inputs)
    res = run_bass_kernel_spmd(nc, maps, core_ids=list(range(NCORES)))
    outp = np.concatenate([np.asarray(res.results[k]["out"]) for k in range(NCORES)], 0)
    return outp.reshape(1, 8192, DM)
